# version 04
# baseline (speedup 1.0000x reference)
import numpy as np
import ml_dtypes
from contextlib import ExitStack

import concourse.bass as bass
import concourse.mybir as mybir
from concourse.bass_utils import run_bass_kernel_spmd

F32 = mybir.dt.float32
BF16 = mybir.dt.bfloat16
AF = mybir.ActivationFunctionType
ALU = mybir.AluOpType
AX = mybir.AxisListType

B, S, D, NH, DH = 2, 8192, 1024, 16, 64
E, HID = 8, 4096
TQ = 2048
PNL = 512
NPANEL = S // PNL
NQB = S // PNL
DC = D // 128
NEG = -30000.0


class Sem:
    __slots__ = ("h", "n")

    def __init__(self, h):
        self.h = h
        self.n = 0


class Prog:
    def __init__(self):
        self.ops = {"sync": [], "tensor": [], "vector": [], "scalar": [], "gpsimd": []}

    def op(self, eng, make, waits=(), inc=None, inc_n=1):
        wl = []
        for s, v in waits:
            wl.append((s.h, s.n if v is None else v))
        if inc is not None:
            inc.n += inc_n

        def f(engine, make=make, wl=wl, inc=inc, inc_n=inc_n):
            for h, v in wl:
                if v > 0:
                    engine.wait_ge(h, v)
            ins = make(engine)
            if inc is not None:
                ins.then_inc(inc.h, inc_n)

        self.ops[eng].append(f)


def build_nc(debug=False, stage=4):
    nc = bass.Bass("TRN2", target_bir_lowering=False)

    x_d = nc.dram_tensor("x", [S, D], F32, kind="ExternalInput").ap()
    xq_d = nc.dram_tensor("xq", [TQ, D], F32, kind="ExternalInput").ap()
    wq_d = nc.dram_tensor("wq", [2, D, 128], F32, kind="ExternalInput").ap()
    wk_d = nc.dram_tensor("wk", [2, D, 128], F32, kind="ExternalInput").ap()
    wv_d = nc.dram_tensor("wv", [2, D, 128], F32, kind="ExternalInput").ap()
    bqkv_d = nc.dram_tensor("bqkv", [128, 6], F32, kind="ExternalInput").ap()
    wo_d = nc.dram_tensor("wo", [2, 128, D], F32, kind="ExternalInput").ap()
    bo_d = nc.dram_tensor("bo", [128, D], F32, kind="ExternalInput").ap()
    wg_d = nc.dram_tensor("wg", [D, E], F32, kind="ExternalInput").ap()
    bg_d = nc.dram_tensor("bg", [1, E], F32, kind="ExternalInput").ap()
    w1_d = nc.dram_tensor("w1", [E * D, HID], F32, kind="ExternalInput").ap()
    b1t_d = nc.dram_tensor("b1t", [128, E * 32], F32, kind="ExternalInput").ap()
    w2_d = nc.dram_tensor("w2", [E * HID, D], F32, kind="ExternalInput").ap()
    b2r_d = nc.dram_tensor("b2r", [1, E * D], F32, kind="ExternalInput").ap()
    tri_d = nc.dram_tensor("tri", [128, 128], BF16, kind="ExternalInput").ap()
    trit_d = nc.dram_tensor("trit", [128, 128], BF16, kind="ExternalInput").ap()
    idb_d = nc.dram_tensor("idb", [128, 128], BF16, kind="ExternalInput").ap()
    idf_d = nc.dram_tensor("idf", [128, 128], F32, kind="ExternalInput").ap()
    onesc_d = nc.dram_tensor("onesc", [128, 1], BF16, kind="ExternalInput").ap()
    onesr_d = nc.dram_tensor("onesr", [1, 128], BF16, kind="ExternalInput").ap()
    onesf_d = nc.dram_tensor("onesf", [1, 128], F32, kind="ExternalInput").ap()
    onesw_d = nc.dram_tensor("onesw", [128, 64], BF16, kind="ExternalInput").ap()
    epsc_d = nc.dram_tensor("epsc", [128, 1], F32, kind="ExternalInput").ap()

    y_d = nc.dram_tensor("y", [TQ, D], F32, kind="ExternalOutput").ap()

    sp_d = nc.dram_tensor("s_part", [S, D], F32)
    sr_d = nc.dram_tensor("s_red", [TQ, D], F32)
    w1b_d = nc.dram_tensor("w1bf", [E * D, HID], BF16)
    w2b_d = nc.dram_tensor("w2bf", [E * HID, D], BF16)

    dbg = {}
    if debug:
        for nm, shp, dt in [
            ("dbg_qt", [128, 2 * S], BF16), ("dbg_kt", [128, 2 * S], BF16),
            ("dbg_v", [128, 2 * S], BF16), ("dbg_xnt", [128, DC * TQ], BF16),
            ("dbg_s", [S, D], F32), ("dbg_w", [128, 16 * E], F32),
            ("dbg_g", [128, 8 * 1024], F32), ("dbg_sr", [TQ, D], F32),
            ("dbg_ln", [128, 2048], F32), ("dbg_st", [128, 32], F32),
            ("dbg_xr", [128, 4096], BF16), ("dbg_u0", [128, 6144], F32),
            ("dbg_pt", [128, 2048], BF16),
        ]:
            dbg[nm] = nc.dram_tensor(nm, shp, dt, kind="ExternalOutput").ap()

    st = ExitStack()

    def sb(name, shape, dt):
        return st.enter_context(nc.sbuf_tensor(name, shape, dt))

    def psum(name, shape, dt):
        return st.enter_context(nc.psum_tensor(name, shape, dt))

    arena1 = sb("arena1", [128, 32768], BF16)
    arena2 = sb("arena2", [128, 16384], BF16)
    arena3 = sb("arena3", [128, 8192], F32)
    arena4 = sb("arena4", [128, 8192], BF16)
    xrot = sb("xrot", [128, 4096], BF16)
    pt_sb = sb("pt_sb", [128, 4 * 512], BF16)
    s_sb = sb("s_sb", [128, 1024], F32)
    scr1 = sb("scr1", [128, 1024], F32)
    cbuf = sb("cbuf", [128, 1024], F32)
    x_tile = sb("x_tile", [128, 1024], F32)
    s_tile = sb("s_tile", [128, 1024], F32)
    bo_sb = sb("bo_sb", [128, D], F32)
    rsrow = sb("rsrow", [1, 1024], F32)
    rsb_sb = sb("rsb_sb", [128, 512], F32)
    mht_sb = sb("mht_sb", [128, 2 * 512], BF16)
    nmrow = sb("nmrow", [1, 1024], BF16)
    vT_scr = sb("vT_scr", [128, 512], F32)
    b2row = sb("b2row", [1, 1024], BF16)
    b1t_sb = sb("b1t_sb", [128, E * 32], F32)
    wg_sb = sb("wg_sb", [128, DC * E], BF16)
    gw_sb = sb("gw_sb", [128, 16 * E], F32)
    gsc_sb = sb("gsc_sb", [128, E], F32)
    bqkv_sb = sb("bqkv_sb", [128, 6], F32)
    bqs_sb = sb("bqs_sb", [128, 2], F32)
    bg_sb = sb("bg_sb", [1, E], BF16)
    tri_sb = sb("tri_sb", [128, 128], BF16)
    trit_sb = sb("trit_sb", [128, 128], BF16)
    idb_sb = sb("idb_sb", [128, 128], BF16)
    idf_sb = sb("idf_sb", [128, 128], F32)
    onesc_sb = sb("onesc_sb", [128, 1], BF16)
    onesr_sb = sb("onesr_sb", [1, 128], BF16)
    onesf_sb = sb("onesf_sb", [1, 128], F32)
    onesw_sb = sb("onesw_sb", [128, 64], BF16)
    epsc_sb = sb("epsc_sb", [128, 1], F32)
    stats = sb("stats", [128, 32], F32)

    QT0, KT0 = 0, 16384
    VS = arena2
    XNT = arena2
    XBUF = arena3[:, 0:4096]
    GACC = arena3
    WQ0, WK0, WV0, WO0 = 0, 2048, 4096, 6144
    W1B0, W2B0 = 0, 4096

    pb = [psum(f"pb{i}", [128, 512], F32) for i in range(5)]
    pw = psum("pw", [128, 1024], F32)
    pbT = psum("pbT", [128, 512], F32)

    def sem(name):
        return Sem(st.enter_context(nc.semaphore(name)))

    sx = sem("sx"); swc = sem("swc"); swb = sem("swb")
    sw1 = sem("sw1"); sw2 = sem("sw2")
    pe_proj = sem("pe_proj"); act_proj = sem("act_proj")
    dve_c = sem("dve_c"); s_lns = sem("s_lns"); dve_ln = sem("dve_ln")
    pe_tr = sem("pe_tr"); dve_tr = sem("dve_tr")
    pe_a = sem("pe_a"); dve_a = sem("dve_a")
    dve_nm = sem("dve_nm"); pe_nmt = sem("pe_nmt"); dve_nmr = sem("dve_nmr")
    pe_st = sem("pe_st"); act_st = sem("act_st"); pe_av = sem("pe_av")
    dve_sig = sem("dve_sig"); pe_rsb = sem("pe_rsb"); dve_rsb = sem("dve_rsb")
    dve_mht = sem("dve_mht"); pe_wo = sem("pe_wo"); dve_s = sem("dve_s")
    s_sst = sem("s_sst"); scc = sem("scc")
    pe_g = sem("pe_g"); act_g = sem("act_g"); dve_gp = sem("dve_gp"); dve_g = sem("dve_g")
    pe_h1 = sem("pe_h1"); act_h1 = sem("act_h1")
    pe_m2 = sem("pe_m2"); dve_m2 = sem("dve_m2")
    act_sil = sem("act_sil"); sxl = sem("sxl"); ssl = sem("ssl"); sb2 = sem("sb2")
    dve_z = sem("dve_z"); act_z = sem("act_z")
    dve_out = sem("dve_out"); sout = sem("sout")
    sdbg = sem("sdbg")

    P = Prog()

    def DMA(q, out, in_, waits=(), inc=None):
        P.op(q, lambda EE, o=out, i=in_: EE.dma_start(out=o, in_=i), waits, inc, 16)

    def MM(out, lhsT, rhs, start, stop, tp=None, waits=(), inc=None):
        P.op("tensor",
             lambda EE, o=out, l=lhsT, r=rhs, s1=start, s2=stop, t=tp:
             EE.matmul(o, l, r, start=s1, stop=s2, tile_position=t),
             waits, inc, 1)

    def TRANS(out, in_, ident, waits=(), inc=None):
        P.op("tensor",
             lambda EE, o=out, i=in_, d=ident: EE.transpose(o, i, d),
             waits, inc, 1)

    def ACT(out, in_, func, bias=0.0, scale=1.0, accum=None, waits=(), inc=None):
        P.op("scalar",
             lambda EE, o=out, i=in_, f=func, b=bias, s=scale, a=accum:
             EE.activation(out=o, in_=i, func=f, bias=b, scale=s, accum_out=a),
             waits, inc, 1)

    def V_(meth, *args, waits=(), inc=None, **kw):
        P.op("vector",
             lambda EE, m=meth, a=args, k=kw: getattr(EE, m)(*a, **k),
             waits, inc, 1)

    DMA("gpsimd", arena4.rearrange("p (q r c m) -> p q r c m", q=4, r=2, c=DC)[:, 0],
        wq_d.rearrange("r (c p) m -> p r c m", p=128), inc=swc)
    DMA("gpsimd", arena4.rearrange("p (q r c m) -> p q r c m", q=4, r=2, c=DC)[:, 1],
        wk_d.rearrange("r (c p) m -> p r c m", p=128), inc=swc)
    DMA("gpsimd", arena4.rearrange("p (q r c m) -> p q r c m", q=4, r=2, c=DC)[:, 2],
        wv_d.rearrange("r (c p) m -> p r c m", p=128), inc=swc)
    DMA("gpsimd", arena4.rearrange("p (q r m) -> p q r m", q=4, r=2)[:, 3],
        wo_d.rearrange("r p m -> p r m"), inc=swc)
    DMA("gpsimd", wg_sb.rearrange("p (c e) -> p c e", c=DC),
        wg_d.rearrange("(c p) e -> p c e", p=128), inc=swc)
    DMA("gpsimd", bg_sb[:, :], bg_d[:, :], inc=swc)
    DMA("sync", bqkv_sb[:, :], bqkv_d[:, :], inc=swc)
    DMA("sync", b1t_sb[:, :], b1t_d[:, :], inc=swc)
    DMA("sync", bo_sb[:, :], bo_d[:, :], inc=swc)
    DMA("sync", tri_sb[:, :], tri_d[:, :], inc=swc)
    DMA("sync", trit_sb[:, :], trit_d[:, :], inc=swc)
    DMA("sync", idb_sb[:, :], idb_d[:, :], inc=swc)
    DMA("sync", idf_sb[:, :], idf_d[:, :], inc=swc)
    DMA("sync", onesc_sb[:, :], onesc_d[:, :], inc=swc)
    DMA("sync", onesr_sb[:, :], onesr_d[:, :], inc=swc)
    DMA("sync", onesf_sb[:, :], onesf_d[:, :], inc=swc)
    DMA("sync", onesw_sb[:, :], onesw_d[:, :], inc=swc)
    DMA("sync", epsc_sb[:, :], epsc_d[:, :], inc=swc)
    P.op("scalar", lambda EE: EE.mul(bqs_sb[:, :], bqkv_sb[:, 0:2], 0.125),
         waits=[(swc, None)], inc=act_proj)

    for e in range(E):
        DMA("gpsimd", w1b_d[e * D:(e + 1) * D, :], w1_d[e * D:(e + 1) * D, :], inc=swb)
        DMA("gpsimd", w2b_d[e * HID:(e + 1) * HID, :], w2_d[e * HID:(e + 1) * HID, :], inc=swb)

    ln_s1 = stats[:, 24:25]
    ln_mean = stats[:, 25:26]
    ln_ss = stats[:, 26:27]
    ln_sq = stats[:, 27:28]
    ln_rs = stats[:, 28:29]

    def emit_ln_tile(xtile_ap, xwaits):
        V_("reduce_sum", ln_s1, xtile_ap, axis=AX.X, waits=xwaits, inc=dve_z)
        V_("tensor_scalar", ln_mean, ln_s1, 1.0 / D, None, op0=ALU.mult,
           waits=[(dve_z, None)], inc=dve_z)
        V_("tensor_scalar", cbuf[:, :], xtile_ap, ln_mean, None, op0=ALU.subtract,
           waits=[(s_lns, None), (dve_z, None)], inc=dve_c)
        P.op("scalar",
             lambda EE: EE.activation(out=scr1[:, :], in_=cbuf[:, :], func=AF.Square,
                                      accum_out=ln_ss),
             waits=[(dve_c, None), (pe_tr, pe_tr.n)], inc=act_z)
        ACT(ln_sq, ln_ss, AF.Sqrt, bias=epsc_sb[:, 0:1], scale=1.0 / D,
            waits=[(act_z, None)], inc=s_lns)
        V_("reciprocal", ln_rs, ln_sq, waits=[(s_lns, None)], inc=dve_z)
        V_("tensor_scalar", scr1[:, :], cbuf[:, :], ln_rs, None, op0=ALU.mult,
           waits=[(dve_z, None)], inc=dve_ln)

    def emit_xn_transpose(dst3d_tt):
        for rnd in range(2):
            for c4 in range(4):
                c = rnd * 4 + c4
                w = []
                if c4 == 0:
                    w = [(dve_ln, None), (dve_tr, dve_tr.n), (swc, swc.n)]
                    if rnd == 0:
                        w.append((dve_sig, dve_sig.n))
                TRANS(pbT[:, c4 * 128:(c4 + 1) * 128], scr1[:, c * 128:(c + 1) * 128],
                      idf_sb[:, :], waits=w, inc=(pe_tr if c4 == 3 else None))
            V_("tensor_copy", dst3d_tt[:, rnd * 4:rnd * 4 + 4, :],
               pbT.rearrange("p (c m) -> p c m", c=4),
               waits=[(pe_tr, None)], inc=dve_tr)

    xrot3 = xrot.rearrange("p (c t) -> p c t", c=DC)
    proj_g = 0
    for t in range(NPANEL):
        DMA("sync", XBUF.rearrange("p (n d) -> p n d", n=4),
            x_d.rearrange("(n p) d -> p n d", p=128)[:, 4 * t:4 * t + 4],
            waits=[(dve_c, 4 * t)], inc=sx)
        for tt in range(4):
            emit_ln_tile(XBUF[:, tt * 1024:(tt + 1) * 1024], [(sx, 16 * (t + 1))])
            if debug and t == 0 and tt == 0:
                DMA("sync", dbg["dbg_ln"][:, 0:1024], scr1[:, :],
                    waits=[(dve_ln, None)], inc=sdbg)
                DMA("sync", dbg["dbg_ln"][:, 1024:2048], cbuf[:, :],
                    waits=[(dve_ln, None)], inc=sdbg)
                DMA("sync", dbg["dbg_st"][:, :], stats[:, :],
                    waits=[(dve_ln, None)], inc=sdbg)
                V_("engine_nop", waits=[(sdbg, None)])
            emit_xn_transpose(xrot3[:, :, tt * 128:(tt + 1) * 128])
        if debug and t == 0:
            DMA("sync", dbg["dbg_xr"][:, :], xrot[:, :],
                waits=[(dve_tr, None)], inc=sdbg)
            V_("engine_nop", waits=[(sdbg, None)])
        for pair in range(2):
            for which, w0col, bcol, scale in [
                ("q", WQ0, 0, 0.125), ("k", WK0, 2, 1.0), ("v", WV0, 4, 1.0),
            ]:
                slot = pb[proj_g % 2]
                for c in range(DC):
                    w = []
                    if c == 0:
                        w = [(dve_tr, None), (act_proj, max(0, act_proj.n - 1)),
                             (swc, None)]
                    MM(slot[:, :],
                       arena4[:, w0col + (pair * DC + c) * 128: w0col + (pair * DC + c + 1) * 128],
                       xrot3[:, c, :], start=(c == 0), stop=(c == DC - 1),
                       waits=w, inc=(pe_proj if c == DC - 1 else None))
                if which == "q":
                    bias_ap = bqs_sb[:, pair:pair + 1]
                else:
                    bias_ap = bqkv_sb[:, bcol + pair:bcol + pair + 1]
                if which == "q":
                    V_("tensor_scalar",
                       arena1[:, QT0 + pair * S + t * PNL: QT0 + pair * S + (t + 1) * PNL],
                       slot[:, :], 0.125, bias_ap, op0=ALU.mult, op1=ALU.add,
                       waits=[(pe_proj, None), (act_proj, 1)], inc=act_proj)
                elif which == "k":
                    V_("tensor_scalar",
                       arena1[:, KT0 + pair * S + t * PNL: KT0 + pair * S + (t + 1) * PNL],
                       slot[:, :], bias_ap, None, op0=ALU.add,
                       waits=[(pe_proj, None)], inc=act_proj)
                else:
                    V_("tensor_scalar", vT_scr[:, :], slot[:, :], bias_ap, None,
                       op0=ALU.add,
                       waits=[(pe_proj, None), (pe_tr, pe_tr.n)], inc=act_proj)
                    for bl in range(4):
                        w = []
                        if bl == 0:
                            w = [(act_proj, None), (dve_tr, dve_tr.n)]
                        TRANS(pbT[:, bl * 128:(bl + 1) * 128],
                              vT_scr[:, bl * 128:(bl + 1) * 128], idf_sb[:, :],
                              waits=w, inc=(pe_tr if bl == 3 else None))
                    V_("tensor_copy",
                       VS[:, pair * S + t * PNL: pair * S + (t + 1) * PNL],
                       pbT[:, 0:512], waits=[(pe_tr, None)], inc=dve_tr)
                proj_g += 1

    PH1_ACT = act_proj.n
    PH1_TR = dve_tr.n

    for sl in range(2):
        DMA("gpsimd",
            arena4[:, sl * 2048:(sl + 1) * 2048].rearrange("p (c h) -> p c h", c=DC),
            w1b_d.rearrange("(ee c p) h -> p ee c h", c=DC, p=128)
            [:, 0, :, sl * 256:(sl + 1) * 256],
            waits=[(swb, None), (act_proj, PH1_ACT)], inc=sw1)

    if debug:
        DMA("sync", dbg["dbg_qt"][:, :], arena1[:, QT0:QT0 + 2 * S],
            waits=[(act_proj, None)], inc=sdbg)
        DMA("sync", dbg["dbg_kt"][:, :], arena1[:, KT0:KT0 + 2 * S],
            waits=[(act_proj, None)], inc=sdbg)
        DMA("sync", dbg["dbg_v"][:, :], VS[:, :], waits=[(dve_tr, None)], inc=sdbg)

    if stage < 2:
        P.op("sync", lambda EE: EE.nop(), waits=[(sout, None), (sdbg, None)])
        return _finish(nc, P, st)
    m_run = stats[:, 0:8]
    tmp8 = stats[:, 8:16]
    nm8 = stats[:, 16:24]

    def qt_ap(plo, phi, cols):
        return arena1[plo:phi, QT0 + cols[0]:QT0 + cols[1]]

    def kt_ap(plo, phi, cols):
        return arena1[plo:phi, KT0 + cols[0]:KT0 + cols[1]]

    a_g = 0
    kcg = 0
    u = 0
    for qb in range(NQB):
        for pair in range(2):
            qbase = pair * S + qb * PNL
            for ip, kb in enumerate(range(qb, -1, -1)):
                diag = kb == qb
                for qt in range(4):
                    nvis = PNL if not diag else 128 * (qt + 1)
                    bA0 = pb[(a_g % 2) * 2]
                    bA1 = pb[(a_g % 2) * 2 + 1]
                    w0 = [(dve_a, max(0, a_g - 1))]
                    if u == 0 and ip == 0 and qt == 0:
                        w0.append((act_proj, PH1_ACT))
                    MM(bA0[:, 0:nvis],
                       qt_ap(0, 64, (qbase + qt * 128, qbase + (qt + 1) * 128)),
                       kt_ap(0, 64, (pair * S + kb * PNL, pair * S + kb * PNL + nvis)),
                       start=True, stop=not diag, tp=(0, 0), waits=w0)
                    if diag:
                        MM(bA0[:, 128 * qt:128 * (qt + 1)], idb_sb[:, :], tri_sb[:, :],
                           start=False, stop=True)
                    MM(bA1[:, 0:nvis],
                       qt_ap(64, 128, (qbase + qt * 128, qbase + (qt + 1) * 128)),
                       kt_ap(64, 128, (pair * S + kb * PNL, pair * S + kb * PNL + nvis)),
                       start=True, stop=not diag, tp=(64, 0),
                       inc=(None if diag else pe_a))
                    if diag:
                        MM(bA1[:, 128 * qt:128 * (qt + 1)], idb_sb[:, :], tri_sb[:, :],
                           start=False, stop=True, inc=pe_a)
                    V_("tensor_reduce", tmp8[:, qt:qt + 1], bA0[:, 0:nvis],
                       axis=AX.X, op=ALU.max, waits=[(pe_a, None)])
                    V_("tensor_reduce", tmp8[:, 4 + qt:5 + qt], bA1[:, 0:nvis],
                       axis=AX.X, op=ALU.max, inc=dve_a)
                    a_g += 1
                if ip == 0:
                    V_("tensor_copy", m_run, tmp8, waits=[(dve_a, None), (dve_z, None)],
                       inc=dve_z)
                else:
                    V_("tensor_tensor", m_run, m_run, tmp8, op=ALU.max,
                       waits=[(dve_a, None), (dve_z, None)], inc=dve_z)
            V_("tensor_scalar", nm8, m_run, -1.0, None, op0=ALU.mult,
               waits=[(dve_z, None)], inc=dve_nm)
            for rnd in range(2):
                for c4 in range(4):
                    col = rnd * 4 + c4
                    w = []
                    if c4 == 0:
                        w = [(dve_nm, None), (dve_nmr, dve_nmr.n),
                             (dve_sig, dve_sig.n)]
                    TRANS(pbT[0:1, c4 * 128:(c4 + 1) * 128], nm8[:, col:col + 1],
                          idf_sb[:, :], waits=w, inc=(pe_nmt if c4 == 3 else None))
                V_("tensor_copy", nmrow[0:1, rnd * 512:(rnd + 1) * 512],
                   pbT[0:1, 0:512], waits=[(pe_nmt, None)], inc=dve_nmr)

            first_k = True
            for kb in range(qb, -1, -1):
                diagp = kb == qb
                for kc in range(4):
                    qlo = 128 * kc if diagp else 0
                    kcol = pair * S + kb * PNL + kc * 128
                    s0 = (kcg % 2) * 2
                    s1 = s0 + 1
                    wst = [(act_st, kcg)]
                    if first_k:
                        wst += [(dve_nmr, None), (dve_s, dve_s.n),
                                (dve_mht, dve_mht.n)]
                    MM(pw[:, qlo:512], kt_ap(0, 64, (kcol, kcol + 128)),
                       qt_ap(0, 64, (qbase + qlo, qbase + PNL)),
                       start=True, stop=False, tp=(0, 0), waits=wst)
                    if diagp:
                        MM(pw[:, qlo:qlo + 128], idb_sb[:, :], trit_sb[:, :],
                           start=False, stop=False)
                    MM(pw[:, qlo:512], onesr_sb[0:1, :], nmrow[0:1, qlo:512],
                       start=False, stop=True)
                    MM(pw[:, 512 + qlo:1024], kt_ap(64, 128, (kcol, kcol + 128)),
                       qt_ap(64, 128, (qbase + qlo, qbase + PNL)),
                       start=True, stop=False, tp=(64, 0))
                    if diagp:
                        MM(pw[:, 512 + qlo:512 + qlo + 128], idb_sb[:, :], trit_sb[:, :],
                           start=False, stop=False)
                    MM(pw[:, 512 + qlo:1024], onesr_sb[0:1, :],
                       nmrow[0:1, 512 + qlo:1024],
                       start=False, stop=True, inc=pe_st)
                    ACT(pt_sb[:, s0 * 512 + qlo:(s0 + 1) * 512], pw[:, qlo:512],
                        AF.Exp, waits=[(pe_st, None), (pe_av, max(0, kcg - 1))])
                    ACT(pt_sb[:, s1 * 512 + qlo:(s1 + 1) * 512], pw[:, 512 + qlo:1024],
                        AF.Exp, inc=act_st)
                    wav = [(act_st, None)]
                    if first_k:
                        wav.append((dve_tr, PH1_TR))
                    vcol = pair * S + kb * PNL + kc * 128
                    MM(pb[4][0:64, qlo:512], VS[:, vcol:vcol + 64],
                       pt_sb[:, s0 * 512 + qlo:(s0 + 1) * 512],
                       start=first_k, stop=False, tp=(0, 0), waits=wav)
                    MM(pb[4][64:128, qlo:512], VS[:, vcol + 64:vcol + 128],
                       pt_sb[:, s1 * 512 + qlo:(s1 + 1) * 512],
                       start=first_k, stop=False, tp=(0, 64))
                    MM(pbT[0:64, qlo:512], onesw_sb[:, :],
                       pt_sb[:, s0 * 512 + qlo:(s0 + 1) * 512],
                       start=first_k, stop=False, tp=(0, 0))
                    MM(pbT[64:128, qlo:512], onesw_sb[:, :],
                       pt_sb[:, s1 * 512 + qlo:(s1 + 1) * 512],
                       start=first_k, stop=False, tp=(0, 64), inc=pe_av)
                    first_k = False
                    kcg += 1
            V_("reciprocal", rsb_sb[:, :], pbT[:, 0:512],
               waits=[(pe_av, None)], inc=dve_sig)
            V_("tensor_tensor", mht_sb[:, pair * 512:(pair + 1) * 512],
               pb[4][:, 0:512], rsb_sb[:, :], op=ALU.mult,
               waits=[(dve_sig, None)], inc=dve_mht)
            if debug and u == 0:
                DMA("gpsimd", dbg["dbg_u0"][:, 0:512], mht_sb[:, 0:512],
                    waits=[(dve_mht, None)], inc=sdbg)
                DMA("sync", dbg["dbg_u0"][:, 512:1024], rsb_sb[:, :],
                    waits=[(dve_rsb, None)], inc=sdbg)
                DMA("sync", dbg["dbg_u0"][:, 1024:1056], stats[:, :],
                    waits=[(dve_nm, None)], inc=sdbg)
                DMA("gpsimd", dbg["dbg_u0"][0:1, 1056:2080], nmrow[:, :],
                    waits=[(dve_nmr, None)], inc=sdbg)
                DMA("sync", dbg["dbg_u0"][0:64, 2080:2592], rsb_sb[0:64, :],
                    waits=[(dve_sig, None)], inc=sdbg)
                DMA("sync", dbg["dbg_pt"][:, :], pt_sb[:, :],
                    waits=[(act_st, None)], inc=sdbg)
                V_("engine_nop", waits=[(sdbg, None)])
                P.op("tensor", lambda EE: EE.nop(), waits=[(sdbg, None)])
            if pair == 1:
                for ts in range(4):
                    for dm in range(2):
                        w = []
                        if ts == 0 and dm == 0:
                            w = [(dve_mht, None), (act_st, act_st.n)]
                        if dm == 0 and ts > 0:
                            w = [(dve_s, dve_s.n)]
                        MM(pw[:, dm * 512:(dm + 1) * 512],
                           mht_sb[:, ts * 128:(ts + 1) * 128],
                           arena4[:, WO0 + dm * 512: WO0 + (dm + 1) * 512],
                           start=True, stop=False, waits=w)
                        MM(pw[:, dm * 512:(dm + 1) * 512],
                           mht_sb[:, 512 + ts * 128: 512 + (ts + 1) * 128],
                           arena4[:, WO0 + 1024 + dm * 512: WO0 + 1024 + (dm + 1) * 512],
                           start=False, stop=True,
                           inc=(pe_wo if dm == 1 else None))
                    V_("tensor_copy", s_sb[:, :], pw[:, 0:1024],
                       waits=[(pe_wo, None), (s_sst, None)], inc=dve_s)
                    DMA("sync",
                        sp_d.ap()[qb * 512 + ts * 128: qb * 512 + (ts + 1) * 128, :],
                        s_sb[:, :], waits=[(dve_s, None)], inc=s_sst)
            u += 1

    if debug:
        DMA("sync", dbg["dbg_s"][:, :], sp_d.ap()[:, :], waits=[(s_sst, None)],
            inc=sdbg)

    if stage < 3:
        P.op("sync", lambda EE: EE.nop(), waits=[(sout, None), (sdbg, None)])
        return _finish(nc, P, st)
    P.op("gpsimd",
         lambda EE: EE.collective_compute(
             "ReduceScatter", ALU.add,
             replica_groups=[[0, 1, 2, 3], [4, 5, 6, 7]],
             ins=[sp_d.ap().opt()], outs=[sr_d.ap().opt()]),
         waits=[(s_sst, None)], inc=scc, inc_n=1)

    xnt3 = XNT.rearrange("p (c t) -> p c t", c=DC)
    for tq in range(4):
        DMA("sync", XBUF.rearrange("p (n d) -> p n d", n=4),
            xq_d.rearrange("(n p) d -> p n d", p=128)[:, 4 * tq:4 * tq + 4],
            waits=[(dve_c, None)], inc=sx)
        for tt in range(4):
            emit_ln_tile(XBUF[:, tt * 1024:(tt + 1) * 1024], [(sx, sx.n)])
            emit_xn_transpose(
                xnt3[:, :, tq * 512 + tt * 128: tq * 512 + (tt + 1) * 128])

    if debug:
        DMA("sync", dbg["dbg_xnt"][:, :], XNT[:, :], waits=[(dve_tr, None)], inc=sdbg)

    for tt in range(16):
        w = [(dve_tr, None), (dve_mht, dve_mht.n)] if tt == 0 else \
            [(act_g, act_g.n), (dve_g, dve_g.n)]
        MM(pb[4][:, 0:E], onesr_sb[0:1, :], bg_sb[0:1, :], start=True, stop=False,
           waits=w)
        for c in range(DC):
            MM(pb[4][:, 0:E], xnt3[:, c, tt * 128:(tt + 1) * 128],
               wg_sb[:, c * E:(c + 1) * E],
               start=False, stop=(c == DC - 1), inc=(pe_g if c == DC - 1 else None))
        V_("tensor_reduce", stats[:, 24:25], pb[4][:, 0:E], axis=AX.X, op=ALU.max,
           negate=True, waits=[(pe_g, None)], inc=dve_gp)
        ACT(gsc_sb[:, :], pb[4][:, 0:E], AF.Exp, bias=stats[:, 24:25],
            accum=stats[:, 25:26], waits=[(dve_gp, None)], inc=act_g)
        V_("reciprocal", stats[:, 26:27], stats[:, 25:26], waits=[(act_g, None)],
           inc=dve_z)
        V_("tensor_scalar", gsc_sb[:, :], gsc_sb[:, :], stats[:, 26:27], None,
           op0=ALU.mult, waits=[(dve_z, None)], inc=dve_z)
        V_("tensor_reduce", stats[:, 24:25], gsc_sb[:, :], axis=AX.X, op=ALU.max,
           waits=[(dve_z, None)], inc=dve_z)
        V_("tensor_scalar", gw_sb[:, tt * E:(tt + 1) * E], gsc_sb[:, :],
           stats[:, 24:25], None, op0=ALU.is_ge, waits=[(dve_z, None)], inc=dve_z)
        V_("scalar_tensor_tensor", gsc_sb[:, :], gw_sb[:, tt * E:(tt + 1) * E],
           -2.0, gsc_sb[:, :], op0=ALU.mult, op1=ALU.add,
           waits=[(dve_z, None)], inc=dve_z)
        V_("tensor_reduce", stats[:, 24:25], gsc_sb[:, :], axis=AX.X, op=ALU.max,
           waits=[(dve_z, None)], inc=dve_z)
        V_("scalar_tensor_tensor", gsc_sb[:, :], gw_sb[:, tt * E:(tt + 1) * E],
           2.0, gsc_sb[:, :], op0=ALU.mult, op1=ALU.add,
           waits=[(dve_z, None)], inc=dve_z)
        V_("tensor_scalar", gw_sb[:, tt * E:(tt + 1) * E], gsc_sb[:, :],
           stats[:, 24:25], None, op0=ALU.is_ge, waits=[(dve_z, None)], inc=dve_z)
        V_("tensor_tensor", gw_sb[:, tt * E:(tt + 1) * E],
           gw_sb[:, tt * E:(tt + 1) * E], gsc_sb[:, :], op=ALU.mult,
           waits=[(dve_z, None)], inc=dve_g)

    if debug:
        DMA("sync", dbg["dbg_w"][:, :], gw_sb[:, :], waits=[(dve_g, None)], inc=sdbg)

    if stage < 4:
        P.op("sync", lambda EE: EE.nop(), waits=[(sout, None), (sdbg, None)])
        return _finish(nc, P, st)
    w1slice = 0
    w2slice = 0
    for tb in range(2):
        for e in range(E):
            DMA("gpsimd", b2row[:, :], b2r_d[0:1, e * D:(e + 1) * D],
                waits=[(pe_m2, pe_m2.n)], inc=sb2)
            for g16 in range(16):
                slot0 = W1B0 + (w1slice % 2) * 2048
                if w1slice >= 2:
                    w = [(swb, None), (pe_h1, max(0, (w1slice - 1) * 4))]
                    DMA("sync",
                        arena4[:, slot0:slot0 + 2048].rearrange("p (c h) -> p c h", c=DC),
                        w1b_d.rearrange("(ee c p) h -> p ee c h", c=DC, p=128)
                        [:, e, :, g16 * 256:(g16 + 1) * 256],
                        waits=w, inc=sw1)
                for hcl in range(2):
                    hc = g16 * 2 + hcl
                    for half in range(2):
                        bidx = (hc * 2 + half) % 4
                        bank = pb[bidx]
                        for d in range(DC):
                            w = []
                            if d == 0:
                                w = [(sw1, sw1.n), (act_h1, max(0, act_h1.n - 3))]
                                if g16 == 0 and hcl == 0 and half == 0:
                                    w.append((dve_m2, dve_m2.n))
                                if tb == 0 and e == 0 and g16 == 0 and hcl == 0 and half == 0:
                                    w.append((dve_g, None))
                            MM(bank[:, :],
                               arena4[:, slot0 + d * 256 + hcl * 128:
                                      slot0 + d * 256 + (hcl + 1) * 128],
                               xnt3[:, d, tb * 1024 + half * 512:
                                    tb * 1024 + (half + 1) * 512],
                               start=(d == 0), stop=(d == DC - 1), waits=w,
                               inc=(pe_h1 if d == DC - 1 else None))
                        ACT(arena1[:, hc * 1024 + half * 512: hc * 1024 + (half + 1) * 512],
                            bank[:, :], AF.Gelu,
                            bias=b1t_sb[:, e * 32 + hc:e * 32 + hc + 1],
                            waits=[(pe_h1, None)], inc=act_h1)
                w1slice += 1
            for st_i in range(2):
                pt_map = [pb[0][:, :], pb[1][:, :], pb[2][:, :], pb[3][:, :],
                          pb[4][:, :], pw[:, 0:512], pw[:, 512:1024], pbT[:, :]]
                for ts_l in range(4):
                    for dm in range(2):
                        tile = pt_map[ts_l * 2 + dm]
                        w = []
                        if ts_l == 0 and dm == 0:
                            w = [(act_h1, None), (dve_m2, dve_m2.n),
                                 (dve_s, dve_s.n), (dve_mht, dve_mht.n),
                                 (sb2, sb2.n)]
                        MM(tile, onesr_sb[0:1, :],
                           b2row[0:1, dm * 512:(dm + 1) * 512],
                           start=True, stop=False, waits=w)
                for hc2 in range(16):
                    slot2 = W2B0 + (w2slice % 2) * 2048
                    w = [(swb, None), (pe_m2, max(0, w2slice - 1))]
                    if w2slice == 0:
                        w.append((pe_wo, pe_wo.n))
                    DMA("sync",
                        arena4[:, slot2:slot2 + 2048].rearrange("p (c m) -> p c m", c=2),
                        w2b_d.rearrange("(ee c p) m -> p ee c m", c=32, p=128)
                        [:, e, hc2 * 2:hc2 * 2 + 2, :],
                        waits=w, inc=sw2)
                    for hh in range(2):
                        hc = hc2 * 2 + hh
                        for ts_l in range(4):
                            ts = st_i * 4 + ts_l
                            for dm in range(2):
                                tile = pt_map[ts_l * 2 + dm]
                                w = []
                                if hh == 0 and ts_l == 0 and dm == 0:
                                    w = [(sw2, sw2.n)]
                                MM(tile,
                                   arena1[:, hc * 1024 + ts * 128: hc * 1024 + (ts + 1) * 128],
                                   arena4[:, slot2 + hh * 1024 + dm * 512:
                                          slot2 + hh * 1024 + (dm + 1) * 512],
                                   start=False, stop=(hc == 31), waits=w,
                                   inc=(pe_m2 if (hh == 1 and ts_l == 3 and dm == 1)
                                        else None))
                    w2slice += 1
                for ts_l in range(4):
                    ts = st_i * 4 + ts_l
                    ttg = tb * 8 + ts
                    for dm in range(2):
                        tile = pt_map[ts_l * 2 + dm]
                        dst = GACC[:, ts * 1024 + dm * 512: ts * 1024 + (dm + 1) * 512]
                        wcol = gw_sb[:, ttg * E + e: ttg * E + e + 1]
                        w = [(pe_m2, None)] if (ts_l == 0 and dm == 0) else []
                        if e == 0:
                            V_("tensor_scalar", dst, tile, wcol, None, op0=ALU.mult,
                               waits=w, inc=dve_m2)
                        else:
                            V_("scalar_tensor_tensor", dst, tile, wcol, dst,
                               op0=ALU.mult, op1=ALU.add, waits=w, inc=dve_m2)
        for ts in range(8):
            ACT(scr1[:, :], GACC[:, ts * 1024:(ts + 1) * 1024], AF.Silu,
                waits=[(dve_m2, None), (dve_out, dve_out.n)], inc=act_sil)
            row0 = tb * 1024 + ts * 128
            DMA("sync", x_tile[:, :], xq_d[row0:row0 + 128, :],
                waits=[(dve_out, dve_out.n)], inc=sxl)
            DMA("gpsimd", s_tile[:, :], sr_d.ap()[row0:row0 + 128, :],
                waits=[(scc, 1), (dve_out, dve_out.n)], inc=ssl)
            V_("tensor_tensor", cbuf[:, :], s_tile[:, :], bo_sb[:, :], op=ALU.add,
               waits=[(ssl, None), (act_sil, None), (sout, sout.n)], inc=dve_z)
            V_("tensor_tensor", cbuf[:, :], cbuf[:, :], scr1[:, :], op=ALU.mult,
               waits=[(dve_z, None)], inc=dve_z)
            V_("tensor_tensor", cbuf[:, :], cbuf[:, :], x_tile[:, :], op=ALU.add,
               waits=[(dve_z, None), (sxl, None)], inc=dve_out)
            DMA("sync", y_d[row0:row0 + 128, :], cbuf[:, :],
                waits=[(dve_out, None)], inc=sout)

    if debug:
        DMA("sync", dbg["dbg_g"][:, :], GACC[:, :], waits=[(dve_m2, None)], inc=sdbg)
        DMA("gpsimd", dbg["dbg_sr"][:, :], sr_d.ap()[:, :], waits=[(scc, 1)], inc=sdbg)

    P.op("sync", lambda EE: EE.nop(), waits=[(sout, None), (sdbg, None)])
    return _finish(nc, P, st)


def _finish(nc, P, st):
    with nc.Block() as block:
        block.sync(lambda EE, fs=tuple(P.ops["sync"]): [f(EE) for f in fs])
        block.tensor(lambda EE, fs=tuple(P.ops["tensor"]): [f(EE) for f in fs])
        block.vector(lambda EE, fs=tuple(P.ops["vector"]): [f(EE) for f in fs])
        block.scalar(lambda EE, fs=tuple(P.ops["scalar"]): [f(EE) for f in fs])
        block.gpsimd(lambda EE, fs=tuple(P.ops["gpsimd"]): [f(EE) for f in fs])
    st.close()
    return nc


def stage_inputs(inputs):
    x = np.asarray(inputs["L0_attn_residual"], np.float32)
    WQ = np.asarray(inputs["L0_attn_W_Q"], np.float32)
    WK = np.asarray(inputs["L0_attn_W_K"], np.float32)
    WV = np.asarray(inputs["L0_attn_W_V"], np.float32)
    WO = np.asarray(inputs["L0_attn_W_O"], np.float32)
    bQ = np.asarray(inputs["L0_attn_b_Q"], np.float32)
    bK = np.asarray(inputs["L0_attn_b_K"], np.float32)
    bV = np.asarray(inputs["L0_attn_b_V"], np.float32)
    bO = np.asarray(inputs["L0_attn_b_O"], np.float32)
    Wg = np.asarray(inputs["L0_ffn_Wg"], np.float32)
    bg = np.asarray(inputs["L0_ffn_bg"], np.float32)
    W1 = np.asarray(inputs["L0_ffn_W1"], np.float32)
    b1 = np.asarray(inputs["L0_ffn_b1"], np.float32)
    W2 = np.asarray(inputs["L0_ffn_W2"], np.float32)
    b2 = np.asarray(inputs["L0_ffn_b2"], np.float32)

    bf = ml_dtypes.bfloat16
    tri = np.zeros((128, 128), np.float32)
    for q in range(128):
        tri[q, q + 1:] = NEG
    ident = np.eye(128)
    common = {
        "wg": np.ascontiguousarray(Wg),
        "bg": np.ascontiguousarray(bg.reshape(1, E)),
        "w1": np.ascontiguousarray(W1.reshape(E * D, HID)),
        "b1t": np.ascontiguousarray(
            b1.reshape(E, 32, 128).transpose(2, 0, 1).reshape(128, E * 32)),
        "w2": np.ascontiguousarray(W2.reshape(E * HID, D)),
        "b2r": np.ascontiguousarray(b2.reshape(1, E * D)),
        "bo": np.ascontiguousarray(np.broadcast_to(bO[None, :], (128, D))),
        "tri": tri.astype(bf),
        "trit": np.ascontiguousarray(tri.T).astype(bf),
        "idb": ident.astype(bf),
        "idf": ident.astype(np.float32),
        "onesc": np.ones((128, 1), bf),
        "onesr": np.ones((1, 128), bf),
        "onesf": np.ones((1, 128), np.float32),
        "onesw": np.ones((128, 64), bf),
        "epsc": np.full((128, 1), 1e-5, np.float32),
    }
    ins_list = []
    for k in range(8):
        b, j = divmod(k, 4)
        hs = [4 * j + i for i in range(4)]
        wq = np.stack([np.concatenate([WQ[hs[2 * p]], WQ[hs[2 * p + 1]]], axis=1)
                       for p in range(2)])
        wk = np.stack([np.concatenate([WK[hs[2 * p]], WK[hs[2 * p + 1]]], axis=1)
                       for p in range(2)])
        wv = np.stack([np.concatenate([WV[hs[2 * p]], WV[hs[2 * p + 1]]], axis=1)
                       for p in range(2)])
        bqkv = np.stack(
            [np.concatenate([bQ[hs[0]], bQ[hs[1]]]),
             np.concatenate([bQ[hs[2]], bQ[hs[3]]]),
             np.concatenate([bK[hs[0]], bK[hs[1]]]),
             np.concatenate([bK[hs[2]], bK[hs[3]]]),
             np.concatenate([bV[hs[0]], bV[hs[1]]]),
             np.concatenate([bV[hs[2]], bV[hs[3]]])], axis=1)
        ins = dict(common)
        ins.update({
            "x": np.ascontiguousarray(x[b]),
            "xq": np.ascontiguousarray(x[b, 2048 * j:2048 * (j + 1)]),
            "wq": np.ascontiguousarray(wq), "wk": np.ascontiguousarray(wk),
            "wv": np.ascontiguousarray(wv),
            "bqkv": np.ascontiguousarray(bqkv),
            "wo": np.ascontiguousarray(WO[256 * j:256 * (j + 1)].reshape(2, 128, D)),
        })
        ins_list.append(ins)
    return ins_list


_NC_CACHE = {}


def kernel(**inputs):
    ins_list = stage_inputs(inputs)
    if "nc" not in _NC_CACHE:
        _NC_CACHE["nc"] = build_nc()
    nc = _NC_CACHE["nc"]
    res = run_bass_kernel_spmd(nc, ins_list, core_ids=list(range(8)))
    out = np.empty((B, S, D), np.float32)
    for k in range(8):
        b, j = divmod(k, 4)
        out[b, 2048 * j:2048 * (j + 1)] = res.results[k]["y"]
    return out


# version 05 (final)
# speedup vs baseline: 1.0267x; 1.0267x over previous
import numpy as np
import ml_dtypes
from contextlib import ExitStack

import concourse.bass as bass
import concourse.mybir as mybir
from concourse.bass_utils import run_bass_kernel_spmd

F32 = mybir.dt.float32
BF16 = mybir.dt.bfloat16
AF = mybir.ActivationFunctionType
ALU = mybir.AluOpType
AX = mybir.AxisListType

B, S, D, NH, DH = 2, 8192, 1024, 16, 64
E, HID = 8, 4096
TQ = 2048
PNL = 512
NPANEL = S // PNL
NQB = S // PNL
DC = D // 128
NEG = -30000.0


class Sem:
    __slots__ = ("h", "n")

    def __init__(self, h):
        self.h = h
        self.n = 0


class Prog:
    def __init__(self):
        self.ops = {"sync": [], "tensor": [], "vector": [], "scalar": [], "gpsimd": []}

    def op(self, eng, make, waits=(), inc=None, inc_n=1):
        wl = []
        for s, v in waits:
            wl.append((s.h, s.n if v is None else v))
        if inc is not None:
            inc.n += inc_n

        def f(engine, make=make, wl=wl, inc=inc, inc_n=inc_n):
            for h, v in wl:
                if v > 0:
                    engine.wait_ge(h, v)
            ins = make(engine)
            if inc is not None:
                ins.then_inc(inc.h, inc_n)

        self.ops[eng].append(f)


def build_nc(debug=False, stage=4):
    nc = bass.Bass("TRN2", target_bir_lowering=False)

    x_d = nc.dram_tensor("x", [S, D], F32, kind="ExternalInput").ap()
    xq_d = nc.dram_tensor("xq", [TQ, D], F32, kind="ExternalInput").ap()
    wq_d = nc.dram_tensor("wq", [2, D, 128], F32, kind="ExternalInput").ap()
    wk_d = nc.dram_tensor("wk", [2, D, 128], F32, kind="ExternalInput").ap()
    wv_d = nc.dram_tensor("wv", [2, D, 128], F32, kind="ExternalInput").ap()
    bqkv_d = nc.dram_tensor("bqkv", [128, 6], F32, kind="ExternalInput").ap()
    wo_d = nc.dram_tensor("wo", [2, 128, D], F32, kind="ExternalInput").ap()
    bo_d = nc.dram_tensor("bo", [128, D], F32, kind="ExternalInput").ap()
    wg_d = nc.dram_tensor("wg", [D, E], F32, kind="ExternalInput").ap()
    bg_d = nc.dram_tensor("bg", [1, E], F32, kind="ExternalInput").ap()
    w1_d = nc.dram_tensor("w1", [E * D, HID], F32, kind="ExternalInput").ap()
    b1t_d = nc.dram_tensor("b1t", [128, E * 32], F32, kind="ExternalInput").ap()
    w2_d = nc.dram_tensor("w2", [E * HID, D], F32, kind="ExternalInput").ap()
    b2r_d = nc.dram_tensor("b2r", [1, E * D], F32, kind="ExternalInput").ap()
    tri_d = nc.dram_tensor("tri", [128, 128], BF16, kind="ExternalInput").ap()
    trit_d = nc.dram_tensor("trit", [128, 128], BF16, kind="ExternalInput").ap()
    idb_d = nc.dram_tensor("idb", [128, 128], BF16, kind="ExternalInput").ap()
    idf_d = nc.dram_tensor("idf", [128, 128], F32, kind="ExternalInput").ap()
    onesc_d = nc.dram_tensor("onesc", [128, 1], BF16, kind="ExternalInput").ap()
    onesr_d = nc.dram_tensor("onesr", [1, 128], BF16, kind="ExternalInput").ap()
    onesf_d = nc.dram_tensor("onesf", [1, 128], F32, kind="ExternalInput").ap()
    onesw_d = nc.dram_tensor("onesw", [128, 64], BF16, kind="ExternalInput").ap()
    epsc_d = nc.dram_tensor("epsc", [128, 1], F32, kind="ExternalInput").ap()

    y_d = nc.dram_tensor("y", [TQ, D], F32, kind="ExternalOutput").ap()

    sp_d = nc.dram_tensor("s_part", [S, D], F32)
    sr_d = nc.dram_tensor("s_red", [TQ, D], F32)
    w1b_d = nc.dram_tensor("w1bf", [E * D, HID], BF16)
    w2b_d = nc.dram_tensor("w2bf", [E * HID, D], BF16)

    dbg = {}
    if debug:
        for nm, shp, dt in [
            ("dbg_qt", [128, 2 * S], BF16), ("dbg_kt", [128, 2 * S], BF16),
            ("dbg_v", [128, 2 * S], BF16), ("dbg_xnt", [128, DC * TQ], BF16),
            ("dbg_s", [S, D], F32), ("dbg_w", [128, 16 * E], F32),
            ("dbg_g", [128, 8 * 1024], F32), ("dbg_sr", [TQ, D], F32),
            ("dbg_ln", [128, 2048], F32), ("dbg_st", [128, 32], F32),
            ("dbg_xr", [128, 4096], BF16), ("dbg_u0", [128, 6144], F32),
            ("dbg_pt", [128, 2048], BF16),
        ]:
            dbg[nm] = nc.dram_tensor(nm, shp, dt, kind="ExternalOutput").ap()

    st = ExitStack()

    def sb(name, shape, dt):
        return st.enter_context(nc.sbuf_tensor(name, shape, dt))

    def psum(name, shape, dt):
        return st.enter_context(nc.psum_tensor(name, shape, dt))

    arena1 = sb("arena1", [128, 32768], BF16)
    arena2 = sb("arena2", [128, 16384], BF16)
    arena3 = sb("arena3", [128, 8192], F32)
    arena4 = sb("arena4", [128, 8192], BF16)
    xrot = sb("xrot", [128, 4096], BF16)
    pt_sb = sb("pt_sb", [128, 4 * 512], BF16)
    s_sb = sb("s_sb", [128, 1024], F32)
    scr1 = sb("scr1", [128, 1024], F32)
    cbuf = sb("cbuf", [128, 1024], F32)
    x_tile = sb("x_tile", [128, 1024], F32)
    s_tile = sb("s_tile", [128, 1024], F32)
    bo_sb = sb("bo_sb", [128, D], F32)
    rsrow = sb("rsrow", [1, 1024], F32)
    rsb_sb = sb("rsb_sb", [128, 512], F32)
    mht_sb = sb("mht_sb", [128, 2 * 512], BF16)
    nmrow = sb("nmrow", [1, 1024], BF16)
    vT_scr = sb("vT_scr", [128, 512], F32)
    b2row = sb("b2row", [1, 1024], BF16)
    b1t_sb = sb("b1t_sb", [128, E * 32], F32)
    wg_sb = sb("wg_sb", [128, DC * E], BF16)
    gw_sb = sb("gw_sb", [128, 16 * E], F32)
    gsc_sb = sb("gsc_sb", [128, E], F32)
    bqkv_sb = sb("bqkv_sb", [128, 6], F32)
    bqs_sb = sb("bqs_sb", [128, 2], F32)
    bg_sb = sb("bg_sb", [1, E], BF16)
    tri_sb = sb("tri_sb", [128, 128], BF16)
    trit_sb = sb("trit_sb", [128, 128], BF16)
    idb_sb = sb("idb_sb", [128, 128], BF16)
    idf_sb = sb("idf_sb", [128, 128], F32)
    onesc_sb = sb("onesc_sb", [128, 1], BF16)
    onesr_sb = sb("onesr_sb", [1, 128], BF16)
    onesf_sb = sb("onesf_sb", [1, 128], F32)
    onesw_sb = sb("onesw_sb", [128, 64], BF16)
    epsc_sb = sb("epsc_sb", [128, 1], F32)
    stats = sb("stats", [128, 32], F32)

    QT0, KT0 = 0, 16384
    VS = arena2
    XNT = arena2
    XBUF = arena3[:, 0:4096]
    GACC = arena3
    WQ0, WK0, WV0, WO0 = 0, 2048, 4096, 6144
    W1B0, W2B0 = 0, 4096

    pb = [psum(f"pb{i}", [128, 512], F32) for i in range(5)]
    pw = psum("pw", [128, 1024], F32)
    pbT = psum("pbT", [128, 512], F32)

    def sem(name):
        return Sem(st.enter_context(nc.semaphore(name)))

    sx = sem("sx"); swc = sem("swc"); swb = sem("swb")
    sw1 = sem("sw1"); sw2 = sem("sw2")
    pe_proj = sem("pe_proj"); act_proj = sem("act_proj")
    dve_c = sem("dve_c"); s_lns = sem("s_lns"); dve_ln = sem("dve_ln")
    pe_tr = sem("pe_tr"); dve_tr = sem("dve_tr")
    pe_a = sem("pe_a"); dve_a = sem("dve_a")
    dve_nm = sem("dve_nm"); pe_nmt = sem("pe_nmt"); dve_nmr = sem("dve_nmr")
    pe_st = sem("pe_st"); act_st = sem("act_st"); pe_av = sem("pe_av")
    dve_sig = sem("dve_sig"); pe_rsb = sem("pe_rsb"); dve_rsb = sem("dve_rsb")
    dve_mht = sem("dve_mht"); pe_wo = sem("pe_wo"); dve_s = sem("dve_s")
    s_sst = sem("s_sst"); scc = sem("scc")
    pe_g = sem("pe_g"); act_g = sem("act_g"); dve_gp = sem("dve_gp"); dve_g = sem("dve_g")
    pe_h1 = sem("pe_h1"); act_h1 = sem("act_h1")
    pe_m2 = sem("pe_m2"); dve_m2 = sem("dve_m2")
    act_sil = sem("act_sil"); sxl = sem("sxl"); ssl = sem("ssl"); sb2 = sem("sb2")
    dve_z = sem("dve_z"); act_z = sem("act_z")
    dve_out = sem("dve_out"); sout = sem("sout")
    sdbg = sem("sdbg")

    P = Prog()

    def DMA(q, out, in_, waits=(), inc=None):
        P.op(q, lambda EE, o=out, i=in_: EE.dma_start(out=o, in_=i), waits, inc, 16)

    def MM(out, lhsT, rhs, start, stop, tp=None, waits=(), inc=None):
        P.op("tensor",
             lambda EE, o=out, l=lhsT, r=rhs, s1=start, s2=stop, t=tp:
             EE.matmul(o, l, r, start=s1, stop=s2, tile_position=t),
             waits, inc, 1)

    def TRANS(out, in_, ident, waits=(), inc=None):
        P.op("tensor",
             lambda EE, o=out, i=in_, d=ident: EE.transpose(o, i, d),
             waits, inc, 1)

    def ACT(out, in_, func, bias=0.0, scale=1.0, accum=None, waits=(), inc=None):
        P.op("scalar",
             lambda EE, o=out, i=in_, f=func, b=bias, s=scale, a=accum:
             EE.activation(out=o, in_=i, func=f, bias=b, scale=s, accum_out=a),
             waits, inc, 1)

    def V_(meth, *args, waits=(), inc=None, **kw):
        P.op("vector",
             lambda EE, m=meth, a=args, k=kw: getattr(EE, m)(*a, **k),
             waits, inc, 1)

    DMA("gpsimd", arena4.rearrange("p (q r c m) -> p q r c m", q=4, r=2, c=DC)[:, 0],
        wq_d.rearrange("r (c p) m -> p r c m", p=128), inc=swc)
    DMA("gpsimd", arena4.rearrange("p (q r c m) -> p q r c m", q=4, r=2, c=DC)[:, 1],
        wk_d.rearrange("r (c p) m -> p r c m", p=128), inc=swc)
    DMA("gpsimd", arena4.rearrange("p (q r c m) -> p q r c m", q=4, r=2, c=DC)[:, 2],
        wv_d.rearrange("r (c p) m -> p r c m", p=128), inc=swc)
    DMA("gpsimd", arena4.rearrange("p (q r m) -> p q r m", q=4, r=2)[:, 3],
        wo_d.rearrange("r p m -> p r m"), inc=swc)
    DMA("gpsimd", wg_sb.rearrange("p (c e) -> p c e", c=DC),
        wg_d.rearrange("(c p) e -> p c e", p=128), inc=swc)
    DMA("gpsimd", bg_sb[:, :], bg_d[:, :], inc=swc)
    DMA("sync", bqkv_sb[:, :], bqkv_d[:, :], inc=swc)
    DMA("sync", b1t_sb[:, :], b1t_d[:, :], inc=swc)
    DMA("sync", bo_sb[:, :], bo_d[:, :], inc=swc)
    DMA("sync", tri_sb[:, :], tri_d[:, :], inc=swc)
    DMA("sync", trit_sb[:, :], trit_d[:, :], inc=swc)
    DMA("sync", idb_sb[:, :], idb_d[:, :], inc=swc)
    DMA("sync", idf_sb[:, :], idf_d[:, :], inc=swc)
    DMA("sync", onesc_sb[:, :], onesc_d[:, :], inc=swc)
    DMA("sync", onesr_sb[:, :], onesr_d[:, :], inc=swc)
    DMA("sync", onesf_sb[:, :], onesf_d[:, :], inc=swc)
    DMA("sync", onesw_sb[:, :], onesw_d[:, :], inc=swc)
    DMA("sync", epsc_sb[:, :], epsc_d[:, :], inc=swc)
    P.op("scalar", lambda EE: EE.mul(bqs_sb[:, :], bqkv_sb[:, 0:2], 0.125),
         waits=[(swc, None)], inc=act_proj)


    ln_s1 = stats[:, 24:25]
    ln_mean = stats[:, 25:26]
    ln_ss = stats[:, 26:27]
    ln_sq = stats[:, 27:28]
    ln_rs = stats[:, 28:29]

    def emit_ln_tile(xtile_ap, xwaits):
        V_("reduce_sum", ln_s1, xtile_ap, axis=AX.X, waits=xwaits, inc=dve_z)
        V_("tensor_scalar", ln_mean, ln_s1, 1.0 / D, None, op0=ALU.mult,
           waits=[(dve_z, None)], inc=dve_z)
        V_("tensor_scalar", cbuf[:, :], xtile_ap, ln_mean, None, op0=ALU.subtract,
           waits=[(s_lns, None), (dve_z, None)], inc=dve_c)
        P.op("scalar",
             lambda EE: EE.activation(out=scr1[:, :], in_=cbuf[:, :], func=AF.Square,
                                      accum_out=ln_ss),
             waits=[(dve_c, None), (pe_tr, pe_tr.n)], inc=act_z)
        ACT(ln_sq, ln_ss, AF.Sqrt, bias=epsc_sb[:, 0:1], scale=1.0 / D,
            waits=[(act_z, None)], inc=s_lns)
        V_("reciprocal", ln_rs, ln_sq, waits=[(s_lns, None)], inc=dve_z)
        V_("tensor_scalar", scr1[:, :], cbuf[:, :], ln_rs, None, op0=ALU.mult,
           waits=[(dve_z, None)], inc=dve_ln)

    def emit_xn_transpose(dst3d_tt):
        for rnd in range(2):
            for c4 in range(4):
                c = rnd * 4 + c4
                w = []
                if c4 == 0:
                    w = [(dve_ln, None), (dve_tr, dve_tr.n), (swc, swc.n)]
                    if rnd == 0:
                        w.append((dve_sig, dve_sig.n))
                TRANS(pbT[:, c4 * 128:(c4 + 1) * 128], scr1[:, c * 128:(c + 1) * 128],
                      idf_sb[:, :], waits=w, inc=(pe_tr if c4 == 3 else None))
            V_("tensor_copy", dst3d_tt[:, rnd * 4:rnd * 4 + 4, :],
               pbT.rearrange("p (c m) -> p c m", c=4),
               waits=[(pe_tr, None)], inc=dve_tr)

    xrot3 = xrot.rearrange("p (c t) -> p c t", c=DC)
    proj_g = 0
    for t in range(NPANEL):
        DMA("sync", XBUF.rearrange("p (n d) -> p n d", n=4),
            x_d.rearrange("(n p) d -> p n d", p=128)[:, 4 * t:4 * t + 4],
            waits=[(dve_c, 4 * t)], inc=sx)
        for tt in range(4):
            emit_ln_tile(XBUF[:, tt * 1024:(tt + 1) * 1024], [(sx, 16 * (t + 1))])
            if debug and t == 0 and tt == 0:
                DMA("sync", dbg["dbg_ln"][:, 0:1024], scr1[:, :],
                    waits=[(dve_ln, None)], inc=sdbg)
                DMA("sync", dbg["dbg_ln"][:, 1024:2048], cbuf[:, :],
                    waits=[(dve_ln, None)], inc=sdbg)
                DMA("sync", dbg["dbg_st"][:, :], stats[:, :],
                    waits=[(dve_ln, None)], inc=sdbg)
                V_("engine_nop", waits=[(sdbg, None)])
            emit_xn_transpose(xrot3[:, :, tt * 128:(tt + 1) * 128])
        if debug and t == 0:
            DMA("sync", dbg["dbg_xr"][:, :], xrot[:, :],
                waits=[(dve_tr, None)], inc=sdbg)
            V_("engine_nop", waits=[(sdbg, None)])
        for pair in range(2):
            for which, w0col, bcol, scale in [
                ("q", WQ0, 0, 0.125), ("k", WK0, 2, 1.0), ("v", WV0, 4, 1.0),
            ]:
                slot = pb[proj_g % 2]
                for c in range(DC):
                    w = []
                    if c == 0:
                        w = [(dve_tr, None), (act_proj, max(0, act_proj.n - 1)),
                             (swc, None)]
                    MM(slot[:, :],
                       arena4[:, w0col + (pair * DC + c) * 128: w0col + (pair * DC + c + 1) * 128],
                       xrot3[:, c, :], start=(c == 0), stop=(c == DC - 1),
                       waits=w, inc=(pe_proj if c == DC - 1 else None))
                if which == "q":
                    bias_ap = bqs_sb[:, pair:pair + 1]
                else:
                    bias_ap = bqkv_sb[:, bcol + pair:bcol + pair + 1]
                if which == "q":
                    V_("tensor_scalar",
                       arena1[:, QT0 + pair * S + t * PNL: QT0 + pair * S + (t + 1) * PNL],
                       slot[:, :], 0.125, bias_ap, op0=ALU.mult, op1=ALU.add,
                       waits=[(pe_proj, None), (act_proj, 1)], inc=act_proj)
                elif which == "k":
                    V_("tensor_scalar",
                       arena1[:, KT0 + pair * S + t * PNL: KT0 + pair * S + (t + 1) * PNL],
                       slot[:, :], bias_ap, None, op0=ALU.add,
                       waits=[(pe_proj, None)], inc=act_proj)
                else:
                    V_("tensor_scalar", vT_scr[:, :], slot[:, :], bias_ap, None,
                       op0=ALU.add,
                       waits=[(pe_proj, None), (pe_tr, pe_tr.n)], inc=act_proj)
                    for bl in range(4):
                        w = []
                        if bl == 0:
                            w = [(act_proj, None), (dve_tr, dve_tr.n)]
                        TRANS(pbT[:, bl * 128:(bl + 1) * 128],
                              vT_scr[:, bl * 128:(bl + 1) * 128], idf_sb[:, :],
                              waits=w, inc=(pe_tr if bl == 3 else None))
                    V_("tensor_copy",
                       VS[:, pair * S + t * PNL: pair * S + (t + 1) * PNL],
                       pbT[:, 0:512], waits=[(pe_tr, None)], inc=dve_tr)
                proj_g += 1

    PH1_ACT = act_proj.n
    PH1_TR = dve_tr.n

    for e in range(E):
        DMA("gpsimd", w1b_d[e * D:(e + 1) * D, :], w1_d[e * D:(e + 1) * D, :],
            waits=([(act_proj, PH1_ACT)] if e == 0 else []), inc=swb)
        DMA("gpsimd", w2b_d[e * HID:(e + 1) * HID, :], w2_d[e * HID:(e + 1) * HID, :],
            inc=swb)

    for sl in range(2):
        DMA("gpsimd",
            arena4[:, sl * 2048:(sl + 1) * 2048].rearrange("p (c h) -> p c h", c=DC),
            w1b_d.rearrange("(ee c p) h -> p ee c h", c=DC, p=128)
            [:, 0, :, sl * 256:(sl + 1) * 256],
            waits=[(swb, None), (act_proj, PH1_ACT)], inc=sw1)

    if debug:
        DMA("sync", dbg["dbg_qt"][:, :], arena1[:, QT0:QT0 + 2 * S],
            waits=[(act_proj, None)], inc=sdbg)
        DMA("sync", dbg["dbg_kt"][:, :], arena1[:, KT0:KT0 + 2 * S],
            waits=[(act_proj, None)], inc=sdbg)
        DMA("sync", dbg["dbg_v"][:, :], VS[:, :], waits=[(dve_tr, None)], inc=sdbg)

    if stage < 2:
        P.op("sync", lambda EE: EE.nop(), waits=[(sout, None), (sdbg, None)])
        return _finish(nc, P, st)
    m_run = stats[:, 0:8]
    tmp8 = stats[:, 8:16]
    nm8 = stats[:, 16:24]

    def qt_ap(plo, phi, cols):
        return arena1[plo:phi, QT0 + cols[0]:QT0 + cols[1]]

    def kt_ap(plo, phi, cols):
        return arena1[plo:phi, KT0 + cols[0]:KT0 + cols[1]]

    a_g = 0
    kcg = 0
    u = 0
    for qb in range(NQB):
        for pair in range(2):
            qbase = pair * S + qb * PNL
            for ip, kb in enumerate(range(qb, -1, -1)):
                diag = kb == qb
                for qt in range(4):
                    nvis = PNL if not diag else 128 * (qt + 1)
                    bA0 = pb[(a_g % 2) * 2]
                    bA1 = pb[(a_g % 2) * 2 + 1]
                    w0 = [(dve_a, max(0, a_g - 1))]
                    if u == 0 and ip == 0 and qt == 0:
                        w0.append((act_proj, PH1_ACT))
                    MM(bA0[:, 0:nvis],
                       qt_ap(0, 64, (qbase + qt * 128, qbase + (qt + 1) * 128)),
                       kt_ap(0, 64, (pair * S + kb * PNL, pair * S + kb * PNL + nvis)),
                       start=True, stop=not diag, tp=(0, 0), waits=w0)
                    if diag:
                        MM(bA0[:, 128 * qt:128 * (qt + 1)], idb_sb[:, :], tri_sb[:, :],
                           start=False, stop=True)
                    MM(bA1[:, 0:nvis],
                       qt_ap(64, 128, (qbase + qt * 128, qbase + (qt + 1) * 128)),
                       kt_ap(64, 128, (pair * S + kb * PNL, pair * S + kb * PNL + nvis)),
                       start=True, stop=not diag, tp=(64, 0),
                       inc=(None if diag else pe_a))
                    if diag:
                        MM(bA1[:, 128 * qt:128 * (qt + 1)], idb_sb[:, :], tri_sb[:, :],
                           start=False, stop=True, inc=pe_a)
                    V_("tensor_reduce", tmp8[:, qt:qt + 1], bA0[:, 0:nvis],
                       axis=AX.X, op=ALU.max, waits=[(pe_a, None)])
                    V_("tensor_reduce", tmp8[:, 4 + qt:5 + qt], bA1[:, 0:nvis],
                       axis=AX.X, op=ALU.max, inc=dve_a)
                    a_g += 1
                if ip == 0:
                    V_("tensor_copy", m_run, tmp8, waits=[(dve_a, None), (dve_z, None)],
                       inc=dve_z)
                else:
                    V_("tensor_tensor", m_run, m_run, tmp8, op=ALU.max,
                       waits=[(dve_a, None), (dve_z, None)], inc=dve_z)
            V_("tensor_scalar", nm8, m_run, -1.0, None, op0=ALU.mult,
               waits=[(dve_z, None)], inc=dve_nm)
            for rnd in range(2):
                for c4 in range(4):
                    col = rnd * 4 + c4
                    w = []
                    if c4 == 0:
                        w = [(dve_nm, None), (dve_nmr, dve_nmr.n),
                             (dve_sig, dve_sig.n)]
                    TRANS(pbT[0:1, c4 * 128:(c4 + 1) * 128], nm8[:, col:col + 1],
                          idf_sb[:, :], waits=w, inc=(pe_nmt if c4 == 3 else None))
                V_("tensor_copy", nmrow[0:1, rnd * 512:(rnd + 1) * 512],
                   pbT[0:1, 0:512], waits=[(pe_nmt, None)], inc=dve_nmr)

            first_k = True
            for kb in range(qb, -1, -1):
                diagp = kb == qb
                for kc in range(4):
                    qlo = 128 * kc if diagp else 0
                    kcol = pair * S + kb * PNL + kc * 128
                    s0 = (kcg % 2) * 2
                    s1 = s0 + 1
                    wst = [(act_st, kcg)]
                    if first_k:
                        wst += [(dve_nmr, None), (dve_s, dve_s.n),
                                (dve_mht, dve_mht.n)]
                    MM(pw[:, qlo:512], kt_ap(0, 64, (kcol, kcol + 128)),
                       qt_ap(0, 64, (qbase + qlo, qbase + PNL)),
                       start=True, stop=False, tp=(0, 0), waits=wst)
                    if diagp:
                        MM(pw[:, qlo:qlo + 128], idb_sb[:, :], trit_sb[:, :],
                           start=False, stop=False)
                    MM(pw[:, qlo:512], onesr_sb[0:1, :], nmrow[0:1, qlo:512],
                       start=False, stop=True)
                    MM(pw[:, 512 + qlo:1024], kt_ap(64, 128, (kcol, kcol + 128)),
                       qt_ap(64, 128, (qbase + qlo, qbase + PNL)),
                       start=True, stop=False, tp=(64, 0))
                    if diagp:
                        MM(pw[:, 512 + qlo:512 + qlo + 128], idb_sb[:, :], trit_sb[:, :],
                           start=False, stop=False)
                    MM(pw[:, 512 + qlo:1024], onesr_sb[0:1, :],
                       nmrow[0:1, 512 + qlo:1024],
                       start=False, stop=True, inc=pe_st)
                    ACT(pt_sb[:, s0 * 512 + qlo:(s0 + 1) * 512], pw[:, qlo:512],
                        AF.Exp, waits=[(pe_st, None), (pe_av, max(0, kcg - 1))])
                    ACT(pt_sb[:, s1 * 512 + qlo:(s1 + 1) * 512], pw[:, 512 + qlo:1024],
                        AF.Exp, inc=act_st)
                    wav = [(act_st, None)]
                    if first_k:
                        wav.append((dve_tr, PH1_TR))
                    vcol = pair * S + kb * PNL + kc * 128
                    MM(pb[4][0:64, qlo:512], VS[:, vcol:vcol + 64],
                       pt_sb[:, s0 * 512 + qlo:(s0 + 1) * 512],
                       start=first_k, stop=False, tp=(0, 0), waits=wav)
                    MM(pb[4][64:128, qlo:512], VS[:, vcol + 64:vcol + 128],
                       pt_sb[:, s1 * 512 + qlo:(s1 + 1) * 512],
                       start=first_k, stop=False, tp=(0, 64))
                    MM(pbT[0:64, qlo:512], onesw_sb[:, :],
                       pt_sb[:, s0 * 512 + qlo:(s0 + 1) * 512],
                       start=first_k, stop=False, tp=(0, 0))
                    MM(pbT[64:128, qlo:512], onesw_sb[:, :],
                       pt_sb[:, s1 * 512 + qlo:(s1 + 1) * 512],
                       start=first_k, stop=False, tp=(0, 64), inc=pe_av)
                    first_k = False
                    kcg += 1
            V_("reciprocal", rsb_sb[:, :], pbT[:, 0:512],
               waits=[(pe_av, None)], inc=dve_sig)
            V_("tensor_tensor", mht_sb[:, pair * 512:(pair + 1) * 512],
               pb[4][:, 0:512], rsb_sb[:, :], op=ALU.mult,
               waits=[(dve_sig, None)], inc=dve_mht)
            if debug and u == 0:
                DMA("gpsimd", dbg["dbg_u0"][:, 0:512], mht_sb[:, 0:512],
                    waits=[(dve_mht, None)], inc=sdbg)
                DMA("sync", dbg["dbg_u0"][:, 512:1024], rsb_sb[:, :],
                    waits=[(dve_rsb, None)], inc=sdbg)
                DMA("sync", dbg["dbg_u0"][:, 1024:1056], stats[:, :],
                    waits=[(dve_nm, None)], inc=sdbg)
                DMA("gpsimd", dbg["dbg_u0"][0:1, 1056:2080], nmrow[:, :],
                    waits=[(dve_nmr, None)], inc=sdbg)
                DMA("sync", dbg["dbg_u0"][0:64, 2080:2592], rsb_sb[0:64, :],
                    waits=[(dve_sig, None)], inc=sdbg)
                DMA("sync", dbg["dbg_pt"][:, :], pt_sb[:, :],
                    waits=[(act_st, None)], inc=sdbg)
                V_("engine_nop", waits=[(sdbg, None)])
                P.op("tensor", lambda EE: EE.nop(), waits=[(sdbg, None)])
            if pair == 1:
                for ts in range(4):
                    for dm in range(2):
                        w = []
                        if ts == 0 and dm == 0:
                            w = [(dve_mht, None), (act_st, act_st.n)]
                        if dm == 0 and ts > 0:
                            w = [(dve_s, dve_s.n)]
                        MM(pw[:, dm * 512:(dm + 1) * 512],
                           mht_sb[:, ts * 128:(ts + 1) * 128],
                           arena4[:, WO0 + dm * 512: WO0 + (dm + 1) * 512],
                           start=True, stop=False, waits=w)
                        MM(pw[:, dm * 512:(dm + 1) * 512],
                           mht_sb[:, 512 + ts * 128: 512 + (ts + 1) * 128],
                           arena4[:, WO0 + 1024 + dm * 512: WO0 + 1024 + (dm + 1) * 512],
                           start=False, stop=True,
                           inc=(pe_wo if dm == 1 else None))
                    V_("tensor_copy", s_sb[:, :], pw[:, 0:1024],
                       waits=[(pe_wo, None), (s_sst, None)], inc=dve_s)
                    DMA("sync",
                        sp_d.ap()[qb * 512 + ts * 128: qb * 512 + (ts + 1) * 128, :],
                        s_sb[:, :], waits=[(dve_s, None)], inc=s_sst)
            u += 1

    if debug:
        DMA("sync", dbg["dbg_s"][:, :], sp_d.ap()[:, :], waits=[(s_sst, None)],
            inc=sdbg)

    if stage < 3:
        P.op("sync", lambda EE: EE.nop(), waits=[(sout, None), (sdbg, None)])
        return _finish(nc, P, st)
    P.op("gpsimd",
         lambda EE: EE.collective_compute(
             "ReduceScatter", ALU.add,
             replica_groups=[[0, 1, 2, 3], [4, 5, 6, 7]],
             ins=[sp_d.ap().opt()], outs=[sr_d.ap().opt()]),
         waits=[(s_sst, None)], inc=scc, inc_n=1)

    xnt3 = XNT.rearrange("p (c t) -> p c t", c=DC)
    for tq in range(4):
        DMA("sync", XBUF.rearrange("p (n d) -> p n d", n=4),
            xq_d.rearrange("(n p) d -> p n d", p=128)[:, 4 * tq:4 * tq + 4],
            waits=[(dve_c, None)], inc=sx)
        for tt in range(4):
            emit_ln_tile(XBUF[:, tt * 1024:(tt + 1) * 1024], [(sx, sx.n)])
            emit_xn_transpose(
                xnt3[:, :, tq * 512 + tt * 128: tq * 512 + (tt + 1) * 128])

    if debug:
        DMA("sync", dbg["dbg_xnt"][:, :], XNT[:, :], waits=[(dve_tr, None)], inc=sdbg)

    for tt in range(16):
        w = [(dve_tr, None), (dve_mht, dve_mht.n)] if tt == 0 else \
            [(act_g, act_g.n), (dve_g, dve_g.n)]
        MM(pb[4][:, 0:E], onesr_sb[0:1, :], bg_sb[0:1, :], start=True, stop=False,
           waits=w)
        for c in range(DC):
            MM(pb[4][:, 0:E], xnt3[:, c, tt * 128:(tt + 1) * 128],
               wg_sb[:, c * E:(c + 1) * E],
               start=False, stop=(c == DC - 1), inc=(pe_g if c == DC - 1 else None))
        V_("tensor_reduce", stats[:, 24:25], pb[4][:, 0:E], axis=AX.X, op=ALU.max,
           negate=True, waits=[(pe_g, None)], inc=dve_gp)
        ACT(gsc_sb[:, :], pb[4][:, 0:E], AF.Exp, bias=stats[:, 24:25],
            accum=stats[:, 25:26], waits=[(dve_gp, None)], inc=act_g)
        V_("reciprocal", stats[:, 26:27], stats[:, 25:26], waits=[(act_g, None)],
           inc=dve_z)
        V_("tensor_scalar", gsc_sb[:, :], gsc_sb[:, :], stats[:, 26:27], None,
           op0=ALU.mult, waits=[(dve_z, None)], inc=dve_z)
        V_("tensor_reduce", stats[:, 24:25], gsc_sb[:, :], axis=AX.X, op=ALU.max,
           waits=[(dve_z, None)], inc=dve_z)
        V_("tensor_scalar", gw_sb[:, tt * E:(tt + 1) * E], gsc_sb[:, :],
           stats[:, 24:25], None, op0=ALU.is_ge, waits=[(dve_z, None)], inc=dve_z)
        V_("scalar_tensor_tensor", gsc_sb[:, :], gw_sb[:, tt * E:(tt + 1) * E],
           -2.0, gsc_sb[:, :], op0=ALU.mult, op1=ALU.add,
           waits=[(dve_z, None)], inc=dve_z)
        V_("tensor_reduce", stats[:, 24:25], gsc_sb[:, :], axis=AX.X, op=ALU.max,
           waits=[(dve_z, None)], inc=dve_z)
        V_("scalar_tensor_tensor", gsc_sb[:, :], gw_sb[:, tt * E:(tt + 1) * E],
           2.0, gsc_sb[:, :], op0=ALU.mult, op1=ALU.add,
           waits=[(dve_z, None)], inc=dve_z)
        V_("tensor_scalar", gw_sb[:, tt * E:(tt + 1) * E], gsc_sb[:, :],
           stats[:, 24:25], None, op0=ALU.is_ge, waits=[(dve_z, None)], inc=dve_z)
        V_("tensor_tensor", gw_sb[:, tt * E:(tt + 1) * E],
           gw_sb[:, tt * E:(tt + 1) * E], gsc_sb[:, :], op=ALU.mult,
           waits=[(dve_z, None)], inc=dve_g)

    if debug:
        DMA("sync", dbg["dbg_w"][:, :], gw_sb[:, :], waits=[(dve_g, None)], inc=sdbg)

    if stage < 4:
        P.op("sync", lambda EE: EE.nop(), waits=[(sout, None), (sdbg, None)])
        return _finish(nc, P, st)
    w1slice = 0
    w2slice = 0
    for tb in range(2):
        for e in range(E):
            DMA("gpsimd", b2row[:, :], b2r_d[0:1, e * D:(e + 1) * D],
                waits=[(pe_m2, pe_m2.n)], inc=sb2)
            for g16 in range(16):
                slot0 = W1B0 + (w1slice % 2) * 2048
                if w1slice >= 2:
                    w = [(swb, None), (pe_h1, max(0, (w1slice - 1) * 4))]
                    DMA("sync",
                        arena4[:, slot0:slot0 + 2048].rearrange("p (c h) -> p c h", c=DC),
                        w1b_d.rearrange("(ee c p) h -> p ee c h", c=DC, p=128)
                        [:, e, :, g16 * 256:(g16 + 1) * 256],
                        waits=w, inc=sw1)
                for hcl in range(2):
                    hc = g16 * 2 + hcl
                    for half in range(2):
                        bidx = (hc * 2 + half) % 4
                        bank = pb[bidx]
                        for d in range(DC):
                            w = []
                            if d == 0:
                                w = [(sw1, sw1.n), (act_h1, max(0, act_h1.n - 3))]
                                if g16 == 0 and hcl == 0 and half == 0:
                                    w.append((dve_m2, dve_m2.n))
                                if tb == 0 and e == 0 and g16 == 0 and hcl == 0 and half == 0:
                                    w.append((dve_g, None))
                            MM(bank[:, :],
                               arena4[:, slot0 + d * 256 + hcl * 128:
                                      slot0 + d * 256 + (hcl + 1) * 128],
                               xnt3[:, d, tb * 1024 + half * 512:
                                    tb * 1024 + (half + 1) * 512],
                               start=(d == 0), stop=(d == DC - 1), waits=w,
                               inc=(pe_h1 if d == DC - 1 else None))
                        ACT(arena1[:, hc * 1024 + half * 512: hc * 1024 + (half + 1) * 512],
                            bank[:, :], AF.Gelu,
                            bias=b1t_sb[:, e * 32 + hc:e * 32 + hc + 1],
                            waits=[(pe_h1, None)], inc=act_h1)
                w1slice += 1
            for st_i in range(2):
                pt_map = [pb[0][:, :], pb[1][:, :], pb[2][:, :], pb[3][:, :],
                          pb[4][:, :], pw[:, 0:512], pw[:, 512:1024], pbT[:, :]]
                for ts_l in range(4):
                    for dm in range(2):
                        tile = pt_map[ts_l * 2 + dm]
                        w = []
                        if ts_l == 0 and dm == 0:
                            w = [(act_h1, None), (dve_m2, dve_m2.n),
                                 (dve_s, dve_s.n), (dve_mht, dve_mht.n),
                                 (sb2, sb2.n)]
                        MM(tile, onesr_sb[0:1, :],
                           b2row[0:1, dm * 512:(dm + 1) * 512],
                           start=True, stop=False, waits=w)
                for hc2 in range(16):
                    slot2 = W2B0 + (w2slice % 2) * 2048
                    w = [(swb, None), (pe_m2, max(0, w2slice - 1))]
                    if w2slice == 0:
                        w.append((pe_wo, pe_wo.n))
                    DMA("sync",
                        arena4[:, slot2:slot2 + 2048].rearrange("p (c m) -> p c m", c=2),
                        w2b_d.rearrange("(ee c p) m -> p ee c m", c=32, p=128)
                        [:, e, hc2 * 2:hc2 * 2 + 2, :],
                        waits=w, inc=sw2)
                    for hh in range(2):
                        hc = hc2 * 2 + hh
                        for ts_l in range(4):
                            ts = st_i * 4 + ts_l
                            for dm in range(2):
                                tile = pt_map[ts_l * 2 + dm]
                                w = []
                                if hh == 0 and ts_l == 0 and dm == 0:
                                    w = [(sw2, sw2.n)]
                                MM(tile,
                                   arena1[:, hc * 1024 + ts * 128: hc * 1024 + (ts + 1) * 128],
                                   arena4[:, slot2 + hh * 1024 + dm * 512:
                                          slot2 + hh * 1024 + (dm + 1) * 512],
                                   start=False, stop=(hc == 31), waits=w,
                                   inc=(pe_m2 if (hh == 1 and ts_l == 3 and dm == 1)
                                        else None))
                    w2slice += 1
                for ts_l in range(4):
                    ts = st_i * 4 + ts_l
                    ttg = tb * 8 + ts
                    for dm in range(2):
                        tile = pt_map[ts_l * 2 + dm]
                        dst = GACC[:, ts * 1024 + dm * 512: ts * 1024 + (dm + 1) * 512]
                        wcol = gw_sb[:, ttg * E + e: ttg * E + e + 1]
                        w = [(pe_m2, None)] if (ts_l == 0 and dm == 0) else []
                        if e == 0:
                            V_("tensor_scalar", dst, tile, wcol, None, op0=ALU.mult,
                               waits=w, inc=dve_m2)
                        else:
                            V_("scalar_tensor_tensor", dst, tile, wcol, dst,
                               op0=ALU.mult, op1=ALU.add, waits=w, inc=dve_m2)
        for ts in range(8):
            ACT(scr1[:, :], GACC[:, ts * 1024:(ts + 1) * 1024], AF.Silu,
                waits=[(dve_m2, None), (dve_out, dve_out.n)], inc=act_sil)
            row0 = tb * 1024 + ts * 128
            DMA("sync", x_tile[:, :], xq_d[row0:row0 + 128, :],
                waits=[(dve_out, dve_out.n)], inc=sxl)
            DMA("gpsimd", s_tile[:, :], sr_d.ap()[row0:row0 + 128, :],
                waits=[(scc, 1), (dve_out, dve_out.n)], inc=ssl)
            V_("tensor_tensor", cbuf[:, :], s_tile[:, :], bo_sb[:, :], op=ALU.add,
               waits=[(ssl, None), (act_sil, None), (sout, sout.n)], inc=dve_z)
            V_("tensor_tensor", cbuf[:, :], cbuf[:, :], scr1[:, :], op=ALU.mult,
               waits=[(dve_z, None)], inc=dve_z)
            V_("tensor_tensor", cbuf[:, :], cbuf[:, :], x_tile[:, :], op=ALU.add,
               waits=[(dve_z, None), (sxl, None)], inc=dve_out)
            DMA("sync", y_d[row0:row0 + 128, :], cbuf[:, :],
                waits=[(dve_out, None)], inc=sout)

    if debug:
        DMA("sync", dbg["dbg_g"][:, :], GACC[:, :], waits=[(dve_m2, None)], inc=sdbg)
        DMA("gpsimd", dbg["dbg_sr"][:, :], sr_d.ap()[:, :], waits=[(scc, 1)], inc=sdbg)

    P.op("sync", lambda EE: EE.nop(), waits=[(sout, None), (sdbg, None)])
    return _finish(nc, P, st)


def _finish(nc, P, st):
    with nc.Block() as block:
        block.sync(lambda EE, fs=tuple(P.ops["sync"]): [f(EE) for f in fs])
        block.tensor(lambda EE, fs=tuple(P.ops["tensor"]): [f(EE) for f in fs])
        block.vector(lambda EE, fs=tuple(P.ops["vector"]): [f(EE) for f in fs])
        block.scalar(lambda EE, fs=tuple(P.ops["scalar"]): [f(EE) for f in fs])
        block.gpsimd(lambda EE, fs=tuple(P.ops["gpsimd"]): [f(EE) for f in fs])
    st.close()
    return nc


def stage_inputs(inputs):
    x = np.asarray(inputs["L0_attn_residual"], np.float32)
    WQ = np.asarray(inputs["L0_attn_W_Q"], np.float32)
    WK = np.asarray(inputs["L0_attn_W_K"], np.float32)
    WV = np.asarray(inputs["L0_attn_W_V"], np.float32)
    WO = np.asarray(inputs["L0_attn_W_O"], np.float32)
    bQ = np.asarray(inputs["L0_attn_b_Q"], np.float32)
    bK = np.asarray(inputs["L0_attn_b_K"], np.float32)
    bV = np.asarray(inputs["L0_attn_b_V"], np.float32)
    bO = np.asarray(inputs["L0_attn_b_O"], np.float32)
    Wg = np.asarray(inputs["L0_ffn_Wg"], np.float32)
    bg = np.asarray(inputs["L0_ffn_bg"], np.float32)
    W1 = np.asarray(inputs["L0_ffn_W1"], np.float32)
    b1 = np.asarray(inputs["L0_ffn_b1"], np.float32)
    W2 = np.asarray(inputs["L0_ffn_W2"], np.float32)
    b2 = np.asarray(inputs["L0_ffn_b2"], np.float32)

    bf = ml_dtypes.bfloat16
    tri = np.zeros((128, 128), np.float32)
    for q in range(128):
        tri[q, q + 1:] = NEG
    ident = np.eye(128)
    common = {
        "wg": np.ascontiguousarray(Wg),
        "bg": np.ascontiguousarray(bg.reshape(1, E)),
        "w1": np.ascontiguousarray(W1.reshape(E * D, HID)),
        "b1t": np.ascontiguousarray(
            b1.reshape(E, 32, 128).transpose(2, 0, 1).reshape(128, E * 32)),
        "w2": np.ascontiguousarray(W2.reshape(E * HID, D)),
        "b2r": np.ascontiguousarray(b2.reshape(1, E * D)),
        "bo": np.ascontiguousarray(np.broadcast_to(bO[None, :], (128, D))),
        "tri": tri.astype(bf),
        "trit": np.ascontiguousarray(tri.T).astype(bf),
        "idb": ident.astype(bf),
        "idf": ident.astype(np.float32),
        "onesc": np.ones((128, 1), bf),
        "onesr": np.ones((1, 128), bf),
        "onesf": np.ones((1, 128), np.float32),
        "onesw": np.ones((128, 64), bf),
        "epsc": np.full((128, 1), 1e-5, np.float32),
    }
    ins_list = []
    for k in range(8):
        b, j = divmod(k, 4)
        hs = [4 * j + i for i in range(4)]
        wq = np.stack([np.concatenate([WQ[hs[2 * p]], WQ[hs[2 * p + 1]]], axis=1)
                       for p in range(2)])
        wk = np.stack([np.concatenate([WK[hs[2 * p]], WK[hs[2 * p + 1]]], axis=1)
                       for p in range(2)])
        wv = np.stack([np.concatenate([WV[hs[2 * p]], WV[hs[2 * p + 1]]], axis=1)
                       for p in range(2)])
        bqkv = np.stack(
            [np.concatenate([bQ[hs[0]], bQ[hs[1]]]),
             np.concatenate([bQ[hs[2]], bQ[hs[3]]]),
             np.concatenate([bK[hs[0]], bK[hs[1]]]),
             np.concatenate([bK[hs[2]], bK[hs[3]]]),
             np.concatenate([bV[hs[0]], bV[hs[1]]]),
             np.concatenate([bV[hs[2]], bV[hs[3]]])], axis=1)
        ins = dict(common)
        ins.update({
            "x": np.ascontiguousarray(x[b]),
            "xq": np.ascontiguousarray(x[b, 2048 * j:2048 * (j + 1)]),
            "wq": np.ascontiguousarray(wq), "wk": np.ascontiguousarray(wk),
            "wv": np.ascontiguousarray(wv),
            "bqkv": np.ascontiguousarray(bqkv),
            "wo": np.ascontiguousarray(WO[256 * j:256 * (j + 1)].reshape(2, 128, D)),
        })
        ins_list.append(ins)
    return ins_list


_NC_CACHE = {}


def kernel(**inputs):
    ins_list = stage_inputs(inputs)
    if "nc" not in _NC_CACHE:
        _NC_CACHE["nc"] = build_nc()
    nc = _NC_CACHE["nc"]
    res = run_bass_kernel_spmd(nc, ins_list, core_ids=list(range(8)))
    out = np.empty((B, S, D), np.float32)
    for k in range(8):
        b, j = divmod(k, 4)
        out[b, 2048 * j:2048 * (j + 1)] = res.results[k]["y"]
    return out


# version 06
# speedup vs baseline: 1.0292x; 1.0025x over previous
import numpy as np
import ml_dtypes
from contextlib import ExitStack

import concourse.bass as bass
import concourse.mybir as mybir
from concourse.bass_utils import run_bass_kernel_spmd

F32 = mybir.dt.float32
BF16 = mybir.dt.bfloat16
AF = mybir.ActivationFunctionType
ALU = mybir.AluOpType
AX = mybir.AxisListType

B, S, D, NH, DH = 2, 8192, 1024, 16, 64
E, HID = 8, 4096
TQ = 2048
PNL = 512
NPANEL = S // PNL
NQB = S // PNL
DC = D // 128
NEG = -30000.0


class Sem:
    __slots__ = ("h", "n")

    def __init__(self, h):
        self.h = h
        self.n = 0


class Prog:
    def __init__(self):
        self.ops = {"sync": [], "tensor": [], "vector": [], "scalar": [], "gpsimd": []}

    def op(self, eng, make, waits=(), inc=None, inc_n=1):
        wl = []
        for s, v in waits:
            wl.append((s.h, s.n if v is None else v))
        if inc is not None:
            inc.n += inc_n

        def f(engine, make=make, wl=wl, inc=inc, inc_n=inc_n):
            for h, v in wl:
                if v > 0:
                    engine.wait_ge(h, v)
            ins = make(engine)
            if inc is not None:
                ins.then_inc(inc.h, inc_n)

        self.ops[eng].append(f)


def build_nc(debug=False, stage=4):
    nc = bass.Bass("TRN2", target_bir_lowering=False)

    x_d = nc.dram_tensor("x", [S, D], F32, kind="ExternalInput").ap()
    xq_d = nc.dram_tensor("xq", [TQ, D], F32, kind="ExternalInput").ap()
    wq_d = nc.dram_tensor("wq", [2, D, 128], F32, kind="ExternalInput").ap()
    wk_d = nc.dram_tensor("wk", [2, D, 128], F32, kind="ExternalInput").ap()
    wv_d = nc.dram_tensor("wv", [2, D, 128], F32, kind="ExternalInput").ap()
    bqkv_d = nc.dram_tensor("bqkv", [128, 6], F32, kind="ExternalInput").ap()
    wo_d = nc.dram_tensor("wo", [2, 128, D], F32, kind="ExternalInput").ap()
    bo_d = nc.dram_tensor("bo", [128, D], F32, kind="ExternalInput").ap()
    wg_d = nc.dram_tensor("wg", [D, E], F32, kind="ExternalInput").ap()
    bg_d = nc.dram_tensor("bg", [1, E], F32, kind="ExternalInput").ap()
    w1_d = nc.dram_tensor("w1", [E * D, HID], F32, kind="ExternalInput").ap()
    b1t_d = nc.dram_tensor("b1t", [128, E * 32], F32, kind="ExternalInput").ap()
    w2_d = nc.dram_tensor("w2", [E * HID, D], F32, kind="ExternalInput").ap()
    b2r_d = nc.dram_tensor("b2r", [1, E * D], F32, kind="ExternalInput").ap()
    tri_d = nc.dram_tensor("tri", [128, 128], BF16, kind="ExternalInput").ap()
    trit_d = nc.dram_tensor("trit", [128, 128], BF16, kind="ExternalInput").ap()
    idb_d = nc.dram_tensor("idb", [128, 128], BF16, kind="ExternalInput").ap()
    idf_d = nc.dram_tensor("idf", [128, 128], F32, kind="ExternalInput").ap()
    onesc_d = nc.dram_tensor("onesc", [128, 1], BF16, kind="ExternalInput").ap()
    onesr_d = nc.dram_tensor("onesr", [1, 128], BF16, kind="ExternalInput").ap()
    onesf_d = nc.dram_tensor("onesf", [1, 128], F32, kind="ExternalInput").ap()
    onesw_d = nc.dram_tensor("onesw", [128, 64], BF16, kind="ExternalInput").ap()
    epsc_d = nc.dram_tensor("epsc", [128, 1], F32, kind="ExternalInput").ap()

    y_d = nc.dram_tensor("y", [TQ, D], F32, kind="ExternalOutput").ap()

    sp_d = nc.dram_tensor("s_part", [S, D], F32)
    sr_d = nc.dram_tensor("s_red", [TQ, D], F32)
    w1b_d = nc.dram_tensor("w1bf", [E * D, HID], BF16)
    w2b_d = nc.dram_tensor("w2bf", [E * HID, D], BF16)

    dbg = {}
    if debug:
        for nm, shp, dt in [
            ("dbg_qt", [128, 2 * S], BF16), ("dbg_kt", [128, 2 * S], BF16),
            ("dbg_v", [128, 2 * S], BF16), ("dbg_xnt", [128, DC * TQ], BF16),
            ("dbg_s", [S, D], F32), ("dbg_w", [128, 16 * E], F32),
            ("dbg_g", [128, 8 * 1024], F32), ("dbg_sr", [TQ, D], F32),
            ("dbg_ln", [128, 2048], F32), ("dbg_st", [128, 32], F32),
            ("dbg_xr", [128, 4096], BF16), ("dbg_u0", [128, 6144], F32),
            ("dbg_pt", [128, 2048], BF16),
        ]:
            dbg[nm] = nc.dram_tensor(nm, shp, dt, kind="ExternalOutput").ap()

    st = ExitStack()

    def sb(name, shape, dt):
        return st.enter_context(nc.sbuf_tensor(name, shape, dt))

    def psum(name, shape, dt):
        return st.enter_context(nc.psum_tensor(name, shape, dt))

    arena1 = sb("arena1", [128, 32768], BF16)
    arena2 = sb("arena2", [128, 16384], BF16)
    arena3 = sb("arena3", [128, 8192], F32)
    arena4 = sb("arena4", [128, 8192], BF16)
    xrot = sb("xrot", [128, 4096], BF16)
    pt_sb = sb("pt_sb", [128, 4 * 512], BF16)
    s_sb = sb("s_sb", [128, 1024], F32)
    scr1 = sb("scr1", [128, 1024], F32)
    cbuf = sb("cbuf", [128, 1024], F32)
    x_tile = sb("x_tile", [128, 1024], F32)
    s_tile = sb("s_tile", [128, 1024], F32)
    bo_sb = sb("bo_sb", [128, D], F32)
    rsrow = sb("rsrow", [1, 1024], F32)
    rsb_sb = sb("rsb_sb", [128, 512], F32)
    mht_sb = sb("mht_sb", [128, 2 * 512], BF16)
    nmrow = sb("nmrow", [1, 1024], BF16)
    vT_scr = sb("vT_scr", [128, 512], F32)
    b2row = sb("b2row", [1, 1024], BF16)
    b1t_sb = sb("b1t_sb", [128, E * 32], F32)
    wg_sb = sb("wg_sb", [128, DC * E], BF16)
    gw_sb = sb("gw_sb", [128, 16 * E], F32)
    gsc_sb = sb("gsc_sb", [128, E], F32)
    bqkv_sb = sb("bqkv_sb", [128, 6], F32)
    bqs_sb = sb("bqs_sb", [128, 2], F32)
    bg_sb = sb("bg_sb", [1, E], BF16)
    tri_sb = sb("tri_sb", [128, 128], BF16)
    trit_sb = sb("trit_sb", [128, 128], BF16)
    idb_sb = sb("idb_sb", [128, 128], BF16)
    idf_sb = sb("idf_sb", [128, 128], F32)
    onesc_sb = sb("onesc_sb", [128, 1], BF16)
    onesr_sb = sb("onesr_sb", [1, 128], BF16)
    onesf_sb = sb("onesf_sb", [1, 128], F32)
    onesw_sb = sb("onesw_sb", [128, 64], BF16)
    epsc_sb = sb("epsc_sb", [128, 1], F32)
    stats = sb("stats", [128, 32], F32)

    QT0, KT0 = 0, 16384
    VS = arena2
    XNT = arena2
    XBUF = arena3[:, 0:4096]
    GACC = arena3
    WQ0, WK0, WV0, WO0 = 0, 2048, 4096, 6144
    W1B0, W2B0 = 0, 4096

    pb = [psum(f"pb{i}", [128, 512], F32) for i in range(5)]
    pw = psum("pw", [128, 1024], F32)
    pbT = psum("pbT", [128, 512], F32)

    def sem(name):
        return Sem(st.enter_context(nc.semaphore(name)))

    sx = sem("sx"); swc = sem("swc"); swb = sem("swb")
    sw1 = sem("sw1"); sw2 = sem("sw2"); sw2b = sem("sw2b")
    pe_proj = sem("pe_proj"); act_proj = sem("act_proj")
    dve_c = sem("dve_c"); s_lns = sem("s_lns"); dve_ln = sem("dve_ln")
    pe_tr = sem("pe_tr"); dve_tr = sem("dve_tr")
    pe_a = sem("pe_a"); dve_a = sem("dve_a")
    dve_nm = sem("dve_nm"); pe_nmt = sem("pe_nmt"); dve_nmr = sem("dve_nmr")
    pe_st = sem("pe_st"); act_st = sem("act_st"); pe_av = sem("pe_av")
    dve_sig = sem("dve_sig"); pe_rsb = sem("pe_rsb"); dve_rsb = sem("dve_rsb")
    dve_mht = sem("dve_mht"); pe_wo = sem("pe_wo"); dve_s = sem("dve_s")
    s_sst = sem("s_sst"); scc = sem("scc")
    pe_g = sem("pe_g"); act_g = sem("act_g"); dve_gp = sem("dve_gp"); dve_g = sem("dve_g")
    pe_h1 = sem("pe_h1"); act_h1 = sem("act_h1")
    pe_m2 = sem("pe_m2"); dve_m2 = sem("dve_m2")
    act_sil = sem("act_sil"); sxl = sem("sxl"); ssl = sem("ssl"); sb2 = sem("sb2")
    dve_z = sem("dve_z"); act_z = sem("act_z")
    dve_out = sem("dve_out"); sout = sem("sout")
    sdbg = sem("sdbg")

    P = Prog()

    def DMA(q, out, in_, waits=(), inc=None):
        P.op(q, lambda EE, o=out, i=in_: EE.dma_start(out=o, in_=i), waits, inc, 16)

    def MM(out, lhsT, rhs, start, stop, tp=None, waits=(), inc=None):
        P.op("tensor",
             lambda EE, o=out, l=lhsT, r=rhs, s1=start, s2=stop, t=tp:
             EE.matmul(o, l, r, start=s1, stop=s2, tile_position=t),
             waits, inc, 1)

    def TRANS(out, in_, ident, waits=(), inc=None):
        P.op("tensor",
             lambda EE, o=out, i=in_, d=ident: EE.transpose(o, i, d),
             waits, inc, 1)

    def ACT(out, in_, func, bias=0.0, scale=1.0, accum=None, waits=(), inc=None):
        P.op("scalar",
             lambda EE, o=out, i=in_, f=func, b=bias, s=scale, a=accum:
             EE.activation(out=o, in_=i, func=f, bias=b, scale=s, accum_out=a),
             waits, inc, 1)

    def V_(meth, *args, waits=(), inc=None, **kw):
        P.op("vector",
             lambda EE, m=meth, a=args, k=kw: getattr(EE, m)(*a, **k),
             waits, inc, 1)

    DMA("gpsimd", arena4.rearrange("p (q r c m) -> p q r c m", q=4, r=2, c=DC)[:, 0],
        wq_d.rearrange("r (c p) m -> p r c m", p=128), inc=swc)
    DMA("gpsimd", arena4.rearrange("p (q r c m) -> p q r c m", q=4, r=2, c=DC)[:, 1],
        wk_d.rearrange("r (c p) m -> p r c m", p=128), inc=swc)
    DMA("gpsimd", arena4.rearrange("p (q r c m) -> p q r c m", q=4, r=2, c=DC)[:, 2],
        wv_d.rearrange("r (c p) m -> p r c m", p=128), inc=swc)
    DMA("gpsimd", arena4.rearrange("p (q r m) -> p q r m", q=4, r=2)[:, 3],
        wo_d.rearrange("r p m -> p r m"), inc=swc)
    DMA("gpsimd", wg_sb.rearrange("p (c e) -> p c e", c=DC),
        wg_d.rearrange("(c p) e -> p c e", p=128), inc=swc)
    DMA("gpsimd", bg_sb[:, :], bg_d[:, :], inc=swc)
    DMA("sync", bqkv_sb[:, :], bqkv_d[:, :], inc=swc)
    DMA("sync", b1t_sb[:, :], b1t_d[:, :], inc=swc)
    DMA("sync", bo_sb[:, :], bo_d[:, :], inc=swc)
    DMA("sync", tri_sb[:, :], tri_d[:, :], inc=swc)
    DMA("sync", trit_sb[:, :], trit_d[:, :], inc=swc)
    DMA("sync", idb_sb[:, :], idb_d[:, :], inc=swc)
    DMA("sync", idf_sb[:, :], idf_d[:, :], inc=swc)
    DMA("sync", onesc_sb[:, :], onesc_d[:, :], inc=swc)
    DMA("sync", onesr_sb[:, :], onesr_d[:, :], inc=swc)
    DMA("sync", onesf_sb[:, :], onesf_d[:, :], inc=swc)
    DMA("sync", onesw_sb[:, :], onesw_d[:, :], inc=swc)
    DMA("sync", epsc_sb[:, :], epsc_d[:, :], inc=swc)
    P.op("scalar", lambda EE: EE.mul(bqs_sb[:, :], bqkv_sb[:, 0:2], 0.125),
         waits=[(swc, None)], inc=act_proj)


    ln_s1 = stats[:, 24:25]
    ln_mean = stats[:, 25:26]
    ln_ss = stats[:, 26:27]
    ln_sq = stats[:, 27:28]
    ln_rs = stats[:, 28:29]

    def emit_ln_tile(xtile_ap, xwaits):
        V_("reduce_sum", ln_s1, xtile_ap, axis=AX.X, waits=xwaits, inc=dve_z)
        V_("tensor_scalar", ln_mean, ln_s1, 1.0 / D, None, op0=ALU.mult,
           waits=[(dve_z, None)], inc=dve_z)
        V_("tensor_scalar", cbuf[:, :], xtile_ap, ln_mean, None, op0=ALU.subtract,
           waits=[(s_lns, None), (dve_z, None)], inc=dve_c)
        P.op("scalar",
             lambda EE: EE.activation(out=scr1[:, :], in_=cbuf[:, :], func=AF.Square,
                                      accum_out=ln_ss),
             waits=[(dve_c, None), (pe_tr, pe_tr.n)], inc=act_z)
        ACT(ln_sq, ln_ss, AF.Sqrt, bias=epsc_sb[:, 0:1], scale=1.0 / D,
            waits=[(act_z, None)], inc=s_lns)
        V_("reciprocal", ln_rs, ln_sq, waits=[(s_lns, None)], inc=dve_z)
        V_("tensor_scalar", scr1[:, :], cbuf[:, :], ln_rs, None, op0=ALU.mult,
           waits=[(dve_z, None)], inc=dve_ln)

    def emit_xn_transpose(dst3d_tt):
        for rnd in range(2):
            for c4 in range(4):
                c = rnd * 4 + c4
                w = []
                if c4 == 0:
                    w = [(dve_ln, None), (dve_tr, dve_tr.n), (swc, swc.n)]
                    if rnd == 0:
                        w.append((dve_sig, dve_sig.n))
                TRANS(pbT[:, c4 * 128:(c4 + 1) * 128], scr1[:, c * 128:(c + 1) * 128],
                      idf_sb[:, :], waits=w, inc=(pe_tr if c4 == 3 else None))
            V_("tensor_copy", dst3d_tt[:, rnd * 4:rnd * 4 + 4, :],
               pbT.rearrange("p (c m) -> p c m", c=4),
               waits=[(pe_tr, None)], inc=dve_tr)

    xrot3 = xrot.rearrange("p (c t) -> p c t", c=DC)
    proj_g = 0
    for t in range(NPANEL):
        DMA("sync", XBUF.rearrange("p (n d) -> p n d", n=4),
            x_d.rearrange("(n p) d -> p n d", p=128)[:, 4 * t:4 * t + 4],
            waits=[(dve_c, 4 * t)], inc=sx)
        for tt in range(4):
            emit_ln_tile(XBUF[:, tt * 1024:(tt + 1) * 1024], [(sx, 16 * (t + 1))])
            if debug and t == 0 and tt == 0:
                DMA("sync", dbg["dbg_ln"][:, 0:1024], scr1[:, :],
                    waits=[(dve_ln, None)], inc=sdbg)
                DMA("sync", dbg["dbg_ln"][:, 1024:2048], cbuf[:, :],
                    waits=[(dve_ln, None)], inc=sdbg)
                DMA("sync", dbg["dbg_st"][:, :], stats[:, :],
                    waits=[(dve_ln, None)], inc=sdbg)
                V_("engine_nop", waits=[(sdbg, None)])
            emit_xn_transpose(xrot3[:, :, tt * 128:(tt + 1) * 128])
        if debug and t == 0:
            DMA("sync", dbg["dbg_xr"][:, :], xrot[:, :],
                waits=[(dve_tr, None)], inc=sdbg)
            V_("engine_nop", waits=[(sdbg, None)])
        for pair in range(2):
            for which, w0col, bcol, scale in [
                ("q", WQ0, 0, 0.125), ("k", WK0, 2, 1.0), ("v", WV0, 4, 1.0),
            ]:
                slot = pb[proj_g % 2]
                for c in range(DC):
                    w = []
                    if c == 0:
                        w = [(dve_tr, None), (act_proj, max(0, act_proj.n - 1)),
                             (swc, None)]
                    MM(slot[:, :],
                       arena4[:, w0col + (pair * DC + c) * 128: w0col + (pair * DC + c + 1) * 128],
                       xrot3[:, c, :], start=(c == 0), stop=(c == DC - 1),
                       waits=w, inc=(pe_proj if c == DC - 1 else None))
                if which == "q":
                    bias_ap = bqs_sb[:, pair:pair + 1]
                else:
                    bias_ap = bqkv_sb[:, bcol + pair:bcol + pair + 1]
                if which == "q":
                    V_("tensor_scalar",
                       arena1[:, QT0 + pair * S + t * PNL: QT0 + pair * S + (t + 1) * PNL],
                       slot[:, :], 0.125, bias_ap, op0=ALU.mult, op1=ALU.add,
                       waits=[(pe_proj, None), (act_proj, 1)], inc=act_proj)
                elif which == "k":
                    V_("tensor_scalar",
                       arena1[:, KT0 + pair * S + t * PNL: KT0 + pair * S + (t + 1) * PNL],
                       slot[:, :], bias_ap, None, op0=ALU.add,
                       waits=[(pe_proj, None)], inc=act_proj)
                else:
                    V_("tensor_scalar", vT_scr[:, :], slot[:, :], bias_ap, None,
                       op0=ALU.add,
                       waits=[(pe_proj, None), (pe_tr, pe_tr.n)], inc=act_proj)
                    for bl in range(4):
                        w = []
                        if bl == 0:
                            w = [(act_proj, None), (dve_tr, dve_tr.n)]
                        TRANS(pbT[:, bl * 128:(bl + 1) * 128],
                              vT_scr[:, bl * 128:(bl + 1) * 128], idf_sb[:, :],
                              waits=w, inc=(pe_tr if bl == 3 else None))
                    V_("tensor_copy",
                       VS[:, pair * S + t * PNL: pair * S + (t + 1) * PNL],
                       pbT[:, 0:512], waits=[(pe_tr, None)], inc=dve_tr)
                proj_g += 1

    PH1_ACT = act_proj.n
    PH1_TR = dve_tr.n

    for e in range(E):
        DMA("gpsimd", w1b_d[e * D:(e + 1) * D, :], w1_d[e * D:(e + 1) * D, :],
            waits=([(act_proj, PH1_ACT)] if e == 0 else []), inc=swb)
        DMA("gpsimd", w2b_d[e * HID:(e + 1) * HID, :], w2_d[e * HID:(e + 1) * HID, :],
            inc=swb)

    for sl in range(2):
        DMA("gpsimd",
            arena4[:, sl * 2048:(sl + 1) * 2048].rearrange("p (c h) -> p c h", c=DC),
            w1b_d.rearrange("(ee c p) h -> p ee c h", c=DC, p=128)
            [:, 0, :, sl * 256:(sl + 1) * 256],
            waits=[(swb, None), (act_proj, PH1_ACT)], inc=sw1)

    if debug:
        DMA("sync", dbg["dbg_qt"][:, :], arena1[:, QT0:QT0 + 2 * S],
            waits=[(act_proj, None)], inc=sdbg)
        DMA("sync", dbg["dbg_kt"][:, :], arena1[:, KT0:KT0 + 2 * S],
            waits=[(act_proj, None)], inc=sdbg)
        DMA("sync", dbg["dbg_v"][:, :], VS[:, :], waits=[(dve_tr, None)], inc=sdbg)

    if stage < 2:
        P.op("sync", lambda EE: EE.nop(), waits=[(sout, None), (sdbg, None)])
        return _finish(nc, P, st)
    m_run = stats[:, 0:8]
    tmp8 = stats[:, 8:16]
    nm8 = stats[:, 16:24]

    def qt_ap(plo, phi, cols):
        return arena1[plo:phi, QT0 + cols[0]:QT0 + cols[1]]

    def kt_ap(plo, phi, cols):
        return arena1[plo:phi, KT0 + cols[0]:KT0 + cols[1]]

    a_g = 0
    kcg = 0
    u = 0
    for qb in range(NQB):
        for pair in range(2):
            qbase = pair * S + qb * PNL
            for ip, kb in enumerate(range(qb, -1, -1)):
                diag = kb == qb
                for qt in range(4):
                    nvis = PNL if not diag else 128 * (qt + 1)
                    bA0 = pb[(a_g % 2) * 2]
                    bA1 = pb[(a_g % 2) * 2 + 1]
                    w0 = [(dve_a, max(0, a_g - 1))]
                    if u == 0 and ip == 0 and qt == 0:
                        w0.append((act_proj, PH1_ACT))
                    MM(bA0[:, 0:nvis],
                       qt_ap(0, 64, (qbase + qt * 128, qbase + (qt + 1) * 128)),
                       kt_ap(0, 64, (pair * S + kb * PNL, pair * S + kb * PNL + nvis)),
                       start=True, stop=not diag, tp=(0, 0), waits=w0)
                    if diag:
                        MM(bA0[:, 128 * qt:128 * (qt + 1)], idb_sb[:, :], tri_sb[:, :],
                           start=False, stop=True)
                    MM(bA1[:, 0:nvis],
                       qt_ap(64, 128, (qbase + qt * 128, qbase + (qt + 1) * 128)),
                       kt_ap(64, 128, (pair * S + kb * PNL, pair * S + kb * PNL + nvis)),
                       start=True, stop=not diag, tp=(64, 0),
                       inc=(None if diag else pe_a))
                    if diag:
                        MM(bA1[:, 128 * qt:128 * (qt + 1)], idb_sb[:, :], tri_sb[:, :],
                           start=False, stop=True, inc=pe_a)
                    V_("tensor_reduce", tmp8[:, qt:qt + 1], bA0[:, 0:nvis],
                       axis=AX.X, op=ALU.max, waits=[(pe_a, None)])
                    V_("tensor_reduce", tmp8[:, 4 + qt:5 + qt], bA1[:, 0:nvis],
                       axis=AX.X, op=ALU.max, inc=dve_a)
                    a_g += 1
                if ip == 0:
                    V_("tensor_copy", m_run, tmp8, waits=[(dve_a, None), (dve_z, None)],
                       inc=dve_z)
                else:
                    V_("tensor_tensor", m_run, m_run, tmp8, op=ALU.max,
                       waits=[(dve_a, None), (dve_z, None)], inc=dve_z)
            V_("tensor_scalar", nm8, m_run, -1.0, None, op0=ALU.mult,
               waits=[(dve_z, None)], inc=dve_nm)
            for rnd in range(2):
                for c4 in range(4):
                    col = rnd * 4 + c4
                    w = []
                    if c4 == 0:
                        w = [(dve_nm, None), (dve_nmr, dve_nmr.n),
                             (dve_sig, dve_sig.n)]
                    TRANS(pbT[0:1, c4 * 128:(c4 + 1) * 128], nm8[:, col:col + 1],
                          idf_sb[:, :], waits=w, inc=(pe_nmt if c4 == 3 else None))
                V_("tensor_copy", nmrow[0:1, rnd * 512:(rnd + 1) * 512],
                   pbT[0:1, 0:512], waits=[(pe_nmt, None)], inc=dve_nmr)

            first_k = True
            for kb in range(qb, -1, -1):
                diagp = kb == qb
                for kc in range(4):
                    qlo = 128 * kc if diagp else 0
                    kcol = pair * S + kb * PNL + kc * 128
                    s0 = (kcg % 2) * 2
                    s1 = s0 + 1
                    wst = [(act_st, kcg)]
                    if first_k:
                        wst += [(dve_nmr, None), (dve_s, dve_s.n),
                                (dve_mht, dve_mht.n)]
                    MM(pw[:, qlo:512], kt_ap(0, 64, (kcol, kcol + 128)),
                       qt_ap(0, 64, (qbase + qlo, qbase + PNL)),
                       start=True, stop=False, tp=(0, 0), waits=wst)
                    if diagp:
                        MM(pw[:, qlo:qlo + 128], idb_sb[:, :], trit_sb[:, :],
                           start=False, stop=False)
                    MM(pw[:, qlo:512], onesr_sb[0:1, :], nmrow[0:1, qlo:512],
                       start=False, stop=True)
                    MM(pw[:, 512 + qlo:1024], kt_ap(64, 128, (kcol, kcol + 128)),
                       qt_ap(64, 128, (qbase + qlo, qbase + PNL)),
                       start=True, stop=False, tp=(64, 0))
                    if diagp:
                        MM(pw[:, 512 + qlo:512 + qlo + 128], idb_sb[:, :], trit_sb[:, :],
                           start=False, stop=False)
                    MM(pw[:, 512 + qlo:1024], onesr_sb[0:1, :],
                       nmrow[0:1, 512 + qlo:1024],
                       start=False, stop=True, inc=pe_st)
                    ACT(pt_sb[:, s0 * 512 + qlo:(s0 + 1) * 512], pw[:, qlo:512],
                        AF.Exp, waits=[(pe_st, None), (pe_av, max(0, kcg - 1))])
                    ACT(pt_sb[:, s1 * 512 + qlo:(s1 + 1) * 512], pw[:, 512 + qlo:1024],
                        AF.Exp, inc=act_st)
                    wav = [(act_st, None)]
                    if first_k:
                        wav.append((dve_tr, PH1_TR))
                    vcol = pair * S + kb * PNL + kc * 128
                    MM(pb[4][0:64, qlo:512], VS[:, vcol:vcol + 64],
                       pt_sb[:, s0 * 512 + qlo:(s0 + 1) * 512],
                       start=first_k, stop=False, tp=(0, 0), waits=wav)
                    MM(pb[4][64:128, qlo:512], VS[:, vcol + 64:vcol + 128],
                       pt_sb[:, s1 * 512 + qlo:(s1 + 1) * 512],
                       start=first_k, stop=False, tp=(0, 64))
                    MM(pbT[0:64, qlo:512], onesw_sb[:, :],
                       pt_sb[:, s0 * 512 + qlo:(s0 + 1) * 512],
                       start=first_k, stop=False, tp=(0, 0))
                    MM(pbT[64:128, qlo:512], onesw_sb[:, :],
                       pt_sb[:, s1 * 512 + qlo:(s1 + 1) * 512],
                       start=first_k, stop=False, tp=(0, 64), inc=pe_av)
                    first_k = False
                    kcg += 1
            V_("reciprocal", rsb_sb[:, :], pbT[:, 0:512],
               waits=[(pe_av, None)], inc=dve_sig)
            V_("tensor_tensor", mht_sb[:, pair * 512:(pair + 1) * 512],
               pb[4][:, 0:512], rsb_sb[:, :], op=ALU.mult,
               waits=[(dve_sig, None)], inc=dve_mht)
            if debug and u == 0:
                DMA("gpsimd", dbg["dbg_u0"][:, 0:512], mht_sb[:, 0:512],
                    waits=[(dve_mht, None)], inc=sdbg)
                DMA("sync", dbg["dbg_u0"][:, 512:1024], rsb_sb[:, :],
                    waits=[(dve_rsb, None)], inc=sdbg)
                DMA("sync", dbg["dbg_u0"][:, 1024:1056], stats[:, :],
                    waits=[(dve_nm, None)], inc=sdbg)
                DMA("gpsimd", dbg["dbg_u0"][0:1, 1056:2080], nmrow[:, :],
                    waits=[(dve_nmr, None)], inc=sdbg)
                DMA("sync", dbg["dbg_u0"][0:64, 2080:2592], rsb_sb[0:64, :],
                    waits=[(dve_sig, None)], inc=sdbg)
                DMA("sync", dbg["dbg_pt"][:, :], pt_sb[:, :],
                    waits=[(act_st, None)], inc=sdbg)
                V_("engine_nop", waits=[(sdbg, None)])
                P.op("tensor", lambda EE: EE.nop(), waits=[(sdbg, None)])
            if pair == 1:
                for ts in range(4):
                    for dm in range(2):
                        w = []
                        if ts == 0 and dm == 0:
                            w = [(dve_mht, None), (act_st, act_st.n)]
                        if dm == 0 and ts > 0:
                            w = [(dve_s, dve_s.n)]
                        MM(pw[:, dm * 512:(dm + 1) * 512],
                           mht_sb[:, ts * 128:(ts + 1) * 128],
                           arena4[:, WO0 + dm * 512: WO0 + (dm + 1) * 512],
                           start=True, stop=False, waits=w)
                        MM(pw[:, dm * 512:(dm + 1) * 512],
                           mht_sb[:, 512 + ts * 128: 512 + (ts + 1) * 128],
                           arena4[:, WO0 + 1024 + dm * 512: WO0 + 1024 + (dm + 1) * 512],
                           start=False, stop=True,
                           inc=(pe_wo if dm == 1 else None))
                    V_("tensor_copy", s_sb[:, :], pw[:, 0:1024],
                       waits=[(pe_wo, None), (s_sst, None)], inc=dve_s)
                    DMA("sync",
                        sp_d.ap()[qb * 512 + ts * 128: qb * 512 + (ts + 1) * 128, :],
                        s_sb[:, :], waits=[(dve_s, None)], inc=s_sst)
            u += 1

    if debug:
        DMA("sync", dbg["dbg_s"][:, :], sp_d.ap()[:, :], waits=[(s_sst, None)],
            inc=sdbg)

    if stage < 3:
        P.op("sync", lambda EE: EE.nop(), waits=[(sout, None), (sdbg, None)])
        return _finish(nc, P, st)
    P.op("gpsimd",
         lambda EE: EE.collective_compute(
             "ReduceScatter", ALU.add,
             replica_groups=[[0, 1, 2, 3], [4, 5, 6, 7]],
             ins=[sp_d.ap().opt()], outs=[sr_d.ap().opt()]),
         waits=[(s_sst, None)], inc=scc, inc_n=1)

    xnt3 = XNT.rearrange("p (c t) -> p c t", c=DC)
    for tq in range(4):
        DMA("sync", XBUF.rearrange("p (n d) -> p n d", n=4),
            xq_d.rearrange("(n p) d -> p n d", p=128)[:, 4 * tq:4 * tq + 4],
            waits=[(dve_c, None)], inc=sx)
        for tt in range(4):
            emit_ln_tile(XBUF[:, tt * 1024:(tt + 1) * 1024], [(sx, sx.n)])
            emit_xn_transpose(
                xnt3[:, :, tq * 512 + tt * 128: tq * 512 + (tt + 1) * 128])

    if debug:
        DMA("sync", dbg["dbg_xnt"][:, :], XNT[:, :], waits=[(dve_tr, None)], inc=sdbg)

    for tt in range(16):
        w = [(dve_tr, None), (dve_mht, dve_mht.n)] if tt == 0 else \
            [(act_g, act_g.n), (dve_g, dve_g.n)]
        MM(pb[4][:, 0:E], onesr_sb[0:1, :], bg_sb[0:1, :], start=True, stop=False,
           waits=w)
        for c in range(DC):
            MM(pb[4][:, 0:E], xnt3[:, c, tt * 128:(tt + 1) * 128],
               wg_sb[:, c * E:(c + 1) * E],
               start=False, stop=(c == DC - 1), inc=(pe_g if c == DC - 1 else None))
        V_("tensor_reduce", stats[:, 24:25], pb[4][:, 0:E], axis=AX.X, op=ALU.max,
           negate=True, waits=[(pe_g, None)], inc=dve_gp)
        ACT(gsc_sb[:, :], pb[4][:, 0:E], AF.Exp, bias=stats[:, 24:25],
            accum=stats[:, 25:26], waits=[(dve_gp, None)], inc=act_g)
        V_("reciprocal", stats[:, 26:27], stats[:, 25:26], waits=[(act_g, None)],
           inc=dve_z)
        V_("tensor_scalar", gsc_sb[:, :], gsc_sb[:, :], stats[:, 26:27], None,
           op0=ALU.mult, waits=[(dve_z, None)], inc=dve_z)
        V_("tensor_reduce", stats[:, 24:25], gsc_sb[:, :], axis=AX.X, op=ALU.max,
           waits=[(dve_z, None)], inc=dve_z)
        V_("tensor_scalar", gw_sb[:, tt * E:(tt + 1) * E], gsc_sb[:, :],
           stats[:, 24:25], None, op0=ALU.is_ge, waits=[(dve_z, None)], inc=dve_z)
        V_("scalar_tensor_tensor", gsc_sb[:, :], gw_sb[:, tt * E:(tt + 1) * E],
           -2.0, gsc_sb[:, :], op0=ALU.mult, op1=ALU.add,
           waits=[(dve_z, None)], inc=dve_z)
        V_("tensor_reduce", stats[:, 24:25], gsc_sb[:, :], axis=AX.X, op=ALU.max,
           waits=[(dve_z, None)], inc=dve_z)
        V_("scalar_tensor_tensor", gsc_sb[:, :], gw_sb[:, tt * E:(tt + 1) * E],
           2.0, gsc_sb[:, :], op0=ALU.mult, op1=ALU.add,
           waits=[(dve_z, None)], inc=dve_z)
        V_("tensor_scalar", gw_sb[:, tt * E:(tt + 1) * E], gsc_sb[:, :],
           stats[:, 24:25], None, op0=ALU.is_ge, waits=[(dve_z, None)], inc=dve_z)
        V_("tensor_tensor", gw_sb[:, tt * E:(tt + 1) * E],
           gw_sb[:, tt * E:(tt + 1) * E], gsc_sb[:, :], op=ALU.mult,
           waits=[(dve_z, None)], inc=dve_g)

    if debug:
        DMA("sync", dbg["dbg_w"][:, :], gw_sb[:, :], waits=[(dve_g, None)], inc=sdbg)

    if stage < 4:
        P.op("sync", lambda EE: EE.nop(), waits=[(sout, None), (sdbg, None)])
        return _finish(nc, P, st)
    w1slice = 0
    w2slice = 0
    for tb in range(2):
        for e in range(E):
            DMA("gpsimd", b2row[:, :], b2r_d[0:1, e * D:(e + 1) * D],
                waits=[(pe_m2, pe_m2.n)], inc=sb2)
            for g16 in range(16):
                slot0 = W1B0 + (w1slice % 2) * 2048
                if w1slice >= 2:
                    w = [(swb, None), (pe_h1, max(0, (w1slice - 1) * 4))]
                    DMA("sync",
                        arena4[:, slot0:slot0 + 2048].rearrange("p (c h) -> p c h", c=DC),
                        w1b_d.rearrange("(ee c p) h -> p ee c h", c=DC, p=128)
                        [:, e, :, g16 * 256:(g16 + 1) * 256],
                        waits=w, inc=sw1)
                for hcl in range(2):
                    hc = g16 * 2 + hcl
                    for half in range(2):
                        bidx = (hc * 2 + half) % 4
                        bank = pb[bidx]
                        for d in range(DC):
                            w = []
                            if d == 0:
                                w = [(sw1, sw1.n), (act_h1, max(0, act_h1.n - 3))]
                                if g16 == 0 and hcl == 0 and half == 0:
                                    w.append((dve_m2, dve_m2.n))
                                if tb == 0 and e == 0 and g16 == 0 and hcl == 0 and half == 0:
                                    w.append((dve_g, None))
                            MM(bank[:, :],
                               arena4[:, slot0 + d * 256 + hcl * 128:
                                      slot0 + d * 256 + (hcl + 1) * 128],
                               xnt3[:, d, tb * 1024 + half * 512:
                                    tb * 1024 + (half + 1) * 512],
                               start=(d == 0), stop=(d == DC - 1), waits=w,
                               inc=(pe_h1 if d == DC - 1 else None))
                        ACT(arena1[:, hc * 1024 + half * 512: hc * 1024 + (half + 1) * 512],
                            bank[:, :], AF.Gelu,
                            bias=b1t_sb[:, e * 32 + hc:e * 32 + hc + 1],
                            waits=[(pe_h1, None)], inc=act_h1)
                w1slice += 1
            for st_i in range(2):
                pt_map = [pb[0][:, :], pb[1][:, :], pb[2][:, :], pb[3][:, :],
                          pb[4][:, :], pw[:, 0:512], pw[:, 512:1024], pbT[:, :]]
                for ts_l in range(4):
                    for dm in range(2):
                        tile = pt_map[ts_l * 2 + dm]
                        w = []
                        if ts_l == 0 and dm == 0:
                            w = [(act_h1, None), (dve_m2, dve_m2.n),
                                 (dve_s, dve_s.n), (dve_mht, dve_mht.n),
                                 (sb2, sb2.n)]
                        MM(tile, onesr_sb[0:1, :],
                           b2row[0:1, dm * 512:(dm + 1) * 512],
                           start=True, stop=False, waits=w)
                for hc2 in range(16):
                    slot2 = W2B0 + (w2slice % 2) * 2048
                    w = [(swb, None), (pe_m2, max(0, w2slice - 1))]
                    if w2slice == 0:
                        w.append((pe_wo, pe_wo.n))
                    if st_i == 0 and hc2 < 2:
                        q2, s2 = "sync", sw2
                    else:
                        q2, s2 = "scalar", sw2b
                    DMA(q2,
                        arena4[:, slot2:slot2 + 2048].rearrange("p (c m) -> p c m", c=2),
                        w2b_d.rearrange("(ee c p) m -> p ee c m", c=32, p=128)
                        [:, e, hc2 * 2:hc2 * 2 + 2, :],
                        waits=w, inc=s2)
                    rawwait = (s2, s2.n)
                    for hh in range(2):
                        hc = hc2 * 2 + hh
                        for ts_l in range(4):
                            ts = st_i * 4 + ts_l
                            for dm in range(2):
                                tile = pt_map[ts_l * 2 + dm]
                                w = []
                                if hh == 0 and ts_l == 0 and dm == 0:
                                    w = [rawwait]
                                MM(tile,
                                   arena1[:, hc * 1024 + ts * 128: hc * 1024 + (ts + 1) * 128],
                                   arena4[:, slot2 + hh * 1024 + dm * 512:
                                          slot2 + hh * 1024 + (dm + 1) * 512],
                                   start=False, stop=(hc == 31), waits=w,
                                   inc=(pe_m2 if (hh == 1 and ts_l == 3 and dm == 1)
                                        else None))
                    w2slice += 1
                for ts_l in range(4):
                    ts = st_i * 4 + ts_l
                    ttg = tb * 8 + ts
                    for dm in range(2):
                        tile = pt_map[ts_l * 2 + dm]
                        dst = GACC[:, ts * 1024 + dm * 512: ts * 1024 + (dm + 1) * 512]
                        wcol = gw_sb[:, ttg * E + e: ttg * E + e + 1]
                        w = [(pe_m2, None)] if (ts_l == 0 and dm == 0) else []
                        if e == 0:
                            V_("tensor_scalar", dst, tile, wcol, None, op0=ALU.mult,
                               waits=w, inc=dve_m2)
                        else:
                            V_("scalar_tensor_tensor", dst, tile, wcol, dst,
                               op0=ALU.mult, op1=ALU.add, waits=w, inc=dve_m2)
        for ts in range(8):
            ACT(scr1[:, :], GACC[:, ts * 1024:(ts + 1) * 1024], AF.Silu,
                waits=[(dve_m2, None), (dve_out, dve_out.n)], inc=act_sil)
            row0 = tb * 1024 + ts * 128
            DMA("sync", x_tile[:, :], xq_d[row0:row0 + 128, :],
                waits=[(dve_out, dve_out.n)], inc=sxl)
            DMA("gpsimd", s_tile[:, :], sr_d.ap()[row0:row0 + 128, :],
                waits=[(scc, 1), (dve_out, dve_out.n)], inc=ssl)
            V_("tensor_tensor", cbuf[:, :], s_tile[:, :], bo_sb[:, :], op=ALU.add,
               waits=[(ssl, None), (act_sil, None), (sout, sout.n)], inc=dve_z)
            V_("tensor_tensor", cbuf[:, :], cbuf[:, :], scr1[:, :], op=ALU.mult,
               waits=[(dve_z, None)], inc=dve_z)
            V_("tensor_tensor", cbuf[:, :], cbuf[:, :], x_tile[:, :], op=ALU.add,
               waits=[(dve_z, None), (sxl, None)], inc=dve_out)
            DMA("sync", y_d[row0:row0 + 128, :], cbuf[:, :],
                waits=[(dve_out, None)], inc=sout)

    if debug:
        DMA("sync", dbg["dbg_g"][:, :], GACC[:, :], waits=[(dve_m2, None)], inc=sdbg)
        DMA("gpsimd", dbg["dbg_sr"][:, :], sr_d.ap()[:, :], waits=[(scc, 1)], inc=sdbg)

    P.op("sync", lambda EE: EE.nop(), waits=[(sout, None), (sdbg, None)])
    return _finish(nc, P, st)


def _finish(nc, P, st):
    with nc.Block() as block:
        block.sync(lambda EE, fs=tuple(P.ops["sync"]): [f(EE) for f in fs])
        block.tensor(lambda EE, fs=tuple(P.ops["tensor"]): [f(EE) for f in fs])
        block.vector(lambda EE, fs=tuple(P.ops["vector"]): [f(EE) for f in fs])
        block.scalar(lambda EE, fs=tuple(P.ops["scalar"]): [f(EE) for f in fs])
        block.gpsimd(lambda EE, fs=tuple(P.ops["gpsimd"]): [f(EE) for f in fs])
    st.close()
    return nc


def stage_inputs(inputs):
    x = np.asarray(inputs["L0_attn_residual"], np.float32)
    WQ = np.asarray(inputs["L0_attn_W_Q"], np.float32)
    WK = np.asarray(inputs["L0_attn_W_K"], np.float32)
    WV = np.asarray(inputs["L0_attn_W_V"], np.float32)
    WO = np.asarray(inputs["L0_attn_W_O"], np.float32)
    bQ = np.asarray(inputs["L0_attn_b_Q"], np.float32)
    bK = np.asarray(inputs["L0_attn_b_K"], np.float32)
    bV = np.asarray(inputs["L0_attn_b_V"], np.float32)
    bO = np.asarray(inputs["L0_attn_b_O"], np.float32)
    Wg = np.asarray(inputs["L0_ffn_Wg"], np.float32)
    bg = np.asarray(inputs["L0_ffn_bg"], np.float32)
    W1 = np.asarray(inputs["L0_ffn_W1"], np.float32)
    b1 = np.asarray(inputs["L0_ffn_b1"], np.float32)
    W2 = np.asarray(inputs["L0_ffn_W2"], np.float32)
    b2 = np.asarray(inputs["L0_ffn_b2"], np.float32)

    bf = ml_dtypes.bfloat16
    tri = np.zeros((128, 128), np.float32)
    for q in range(128):
        tri[q, q + 1:] = NEG
    ident = np.eye(128)
    common = {
        "wg": np.ascontiguousarray(Wg),
        "bg": np.ascontiguousarray(bg.reshape(1, E)),
        "w1": np.ascontiguousarray(W1.reshape(E * D, HID)),
        "b1t": np.ascontiguousarray(
            b1.reshape(E, 32, 128).transpose(2, 0, 1).reshape(128, E * 32)),
        "w2": np.ascontiguousarray(W2.reshape(E * HID, D)),
        "b2r": np.ascontiguousarray(b2.reshape(1, E * D)),
        "bo": np.ascontiguousarray(np.broadcast_to(bO[None, :], (128, D))),
        "tri": tri.astype(bf),
        "trit": np.ascontiguousarray(tri.T).astype(bf),
        "idb": ident.astype(bf),
        "idf": ident.astype(np.float32),
        "onesc": np.ones((128, 1), bf),
        "onesr": np.ones((1, 128), bf),
        "onesf": np.ones((1, 128), np.float32),
        "onesw": np.ones((128, 64), bf),
        "epsc": np.full((128, 1), 1e-5, np.float32),
    }
    ins_list = []
    for k in range(8):
        b, j = divmod(k, 4)
        hs = [4 * j + i for i in range(4)]
        wq = np.stack([np.concatenate([WQ[hs[2 * p]], WQ[hs[2 * p + 1]]], axis=1)
                       for p in range(2)])
        wk = np.stack([np.concatenate([WK[hs[2 * p]], WK[hs[2 * p + 1]]], axis=1)
                       for p in range(2)])
        wv = np.stack([np.concatenate([WV[hs[2 * p]], WV[hs[2 * p + 1]]], axis=1)
                       for p in range(2)])
        bqkv = np.stack(
            [np.concatenate([bQ[hs[0]], bQ[hs[1]]]),
             np.concatenate([bQ[hs[2]], bQ[hs[3]]]),
             np.concatenate([bK[hs[0]], bK[hs[1]]]),
             np.concatenate([bK[hs[2]], bK[hs[3]]]),
             np.concatenate([bV[hs[0]], bV[hs[1]]]),
             np.concatenate([bV[hs[2]], bV[hs[3]]])], axis=1)
        ins = dict(common)
        ins.update({
            "x": np.ascontiguousarray(x[b]),
            "xq": np.ascontiguousarray(x[b, 2048 * j:2048 * (j + 1)]),
            "wq": np.ascontiguousarray(wq), "wk": np.ascontiguousarray(wk),
            "wv": np.ascontiguousarray(wv),
            "bqkv": np.ascontiguousarray(bqkv),
            "wo": np.ascontiguousarray(WO[256 * j:256 * (j + 1)].reshape(2, 128, D)),
        })
        ins_list.append(ins)
    return ins_list


_NC_CACHE = {}


def kernel(**inputs):
    ins_list = stage_inputs(inputs)
    if "nc" not in _NC_CACHE:
        _NC_CACHE["nc"] = build_nc()
    nc = _NC_CACHE["nc"]
    res = run_bass_kernel_spmd(nc, ins_list, core_ids=list(range(8)))
    out = np.empty((B, S, D), np.float32)
    for k in range(8):
        b, j = divmod(k, 4)
        out[b, 2048 * j:2048 * (j + 1)] = res.results[k]["y"]
    return out
